# version 01 (final)
# baseline (speedup 1.0000x reference)
import contextlib
import numpy as np
import ml_dtypes
import concourse.bass as bass
import concourse.mybir as mybir
from concourse.bass_utils import run_bass_kernel_spmd

F32 = mybir.dt.float32
BF16 = mybir.dt.bfloat16
I32 = mybir.dt.int32
AF = mybir.ActivationFunctionType
ALU = mybir.AluOpType
AX = mybir.AxisListType

T = 8192
DM = 1024
DI = 2048
DS = 16
D4 = 512
PGM = 4
TT1 = 256
TT2 = 128
TT3 = 512
QT = 2048
HALO = 258
TW = QT + HALO
NL = 6
DH = 64
WIN = 256
DIN2 = 5460
DIN = 2730
GROUPS = [[0, 1, 2, 3], [4, 5, 6, 7]]
EXP20 = float(np.exp(np.float32(-20.0)))
ENGS = ("sync", "vector", "scalar", "tensor", "gpsimd")


def ftiles(n, w=128):
    out = []
    o = 0
    while o < n:
        out.append((o, min(w, n - o)))
        o += w
    return out


A_TILES = ftiles(DIN)
TW_TILES = ftiles(TW, 512)
MAIN_TILES = [(HALO + i * 512, 512) for i in range(4)]


class Sem:
    def __init__(self, es, nc, name):
        self.h = es.enter_context(nc.semaphore(name))
        self.n = 0


class _Chain:

    def __init__(self, h, semh, amt):
        self._h = h
        self._semh = semh
        self._amt = amt
        self.count = 0

    def __getattr__(self, name):
        f = getattr(self._h, name)

        def g(*a, **k):
            ins = f(*a, **k)
            if hasattr(ins, "then_inc"):
                ins.then_inc(self._semh, self._amt)
                self.count += 1
            return ins
        return g


class Prog:
    def __init__(self, nc, es):
        self.nc = nc
        self.block = es.enter_context(nc.Block())
        self.q = {e: [] for e in ENGS}
        self.sems = {e: Sem(es, nc, "s_" + e) for e in ENGS}
        self.dsems = {e: [Sem(es, nc, "d_" + e + "0"), Sem(es, nc, "d_" + e + "1")] for e in ENGS}
        self.dcnt = {e: 0 for e in ENGS}
        self.ccsem = Sem(es, nc, "s_cc")

    def op(self, eng, fn, wait=(), dma=False, cc=False, n=1):
        if dma:
            slot = self.dcnt[eng] % 2
            self.dcnt[eng] += 1
            sem, amt = self.dsems[eng][slot], 16
        elif cc:
            sem, amt = self.ccsem, 1
        else:
            sem, amt = self.sems[eng], 1
        waits = [(s.h, c) for (s, c) in wait]
        if dma or cc:
            if sem.n > 0:
                waits.append((sem.h, sem.n))
        own = self.sems[eng]
        if own.n > 0:
            waits.append((own.h, own.n))

        def run(h):
            seen = set()
            for hh, c in waits:
                key = (id(hh), c)
                if key in seen:
                    continue
                seen.add(key)
                h.wait_ge(hh, c)
            if n == 1:
                fn(h).then_inc(sem.h, amt)
            else:
                prox = _Chain(h, sem.h, amt)
                fn(prox)
                assert prox.count == n, f"op declared n={n} but emitted {prox.count}"
        self.q[eng].append(run)
        sem.n += amt * n
        return (sem, sem.n)

    def raw(self, eng, fn):
        self.q[eng].append(fn)

    def mark(self, eng):
        return (self.sems[eng], self.sems[eng].n)

    def fence(self):
        evs = [self.op(e, lambda h: h.nop()) for e in ENGS]

        def w(h, evs=tuple(evs)):
            for (s, c) in evs:
                h.wait_ge(s.h, c)
        for e in ENGS:
            self.raw(e, w)

    def flush(self):
        for e in ENGS:
            lst = self.q[e]
            if not lst:
                continue
            self.q[e] = []

            def body(h, lst=lst):
                for f in lst:
                    f(h)
            getattr(self.block, e)(body)


def build_program(n_layers=NL, dumps=(), lim=99, m1tiles=None, m2tiles=None, nocc=False):
    nc = bass.Bass("TRN2", target_bir_lowering=False)
    A = {}

    def din(name, shape, dt=F32):
        A[name] = nc.dram_tensor(name, shape, dt, kind="ExternalInput").ap()

    def dint(name, shape, dt=F32):
        A[name] = nc.dram_tensor(name, shape, dt).ap()

    def dout(name, shape, dt=F32):
        A[name] = nc.dram_tensor(name, shape, dt, kind="ExternalOutput").ap()

    din("xT", [DM, T])
    din("winT", [DM, DM])
    din("wxT", [D4, 33])
    din("wdt_r", [PGM, 128]); din("bdt_r", [PGM, 128])
    din("convw_r", [16, 128]); din("convb_r", [PGM, 128])
    din("d_r", [PGM, 128]); din("lnw_r", [PGM, 128]); din("lnb_r", [PGM, 128])
    din("alog", [1, DS])
    din("woutT", [D4, DM])
    din("ident", [128, 128]); din("identb", [128, 128], BF16)
    din("wqkv", [NL, DM, 3 * DM]); din("wo", [NL, DM, DM])
    din("w1", [NL, DM, DIN2]); din("w2", [NL, DIN, DM])
    din("qs_r", [NL, 128]); din("ks_r", [NL, 128])
    din("g_r", [NL, 8, 128]); din("bln_r", [NL, 8, 128]); din("b2_r", [NL, 8, 128])
    din("b1a_r", [NL, 22, 128]); din("b1g_r", [NL, 22, 128])
    for nm in ("cs", "ss", "ci", "si"):
        din(nm, [128, TW], BF16)
    din("masks", [4, 128, 512], BF16)
    din("x_q", [DM, QT])
    din("pmask", [128, 4])
    din("ones2c_in", [128, 2], BF16)
    din("ones2r_in", [2, 128], BF16)

    dint("b_xd_in", [33, T]); dint("b_xd", [33, T])
    dint("b_xdh", [33, T], BF16); dint("b_xdl", [33, T], BF16)
    dint("b_st_in", [2, T]); dint("b_st", [2, T])
    dint("b_zs", [D4, T]); dint("b_y", [D4, T])
    dint("b_mo_in", [4, DM, QT]); dint("b_mo", [DM, QT])
    dint("b_x1", [DM, QT])
    dint("b_ex_in", [DM, HALO]); dint("b_ex_out", [4, DM, HALO])
    dint("b_e2_in", [DM, 1]); dint("b_e2_out", [4, DM, 1])

    dout("o_out", [DM, QT])
    dshapes = {"xi": [D4, T], "xd": [33, T], "y": [D4, T], "mo": [DM, QT],
               "mop": [DM, T], "x1": [DM, QT], "u1": [DM, QT], "zs": [D4, T],
               "stats": [2, T], "u": [DM, QT], "ua": [DM, QT], "umid": [DM, QT], "av": [DM, QT], "qk": [6, 128, TW], "ubd": [DM, TW], "xnd": [DM, QT], "prodd": [128, 22 * 512], "uam": [DM, QT]}
    for nm in set(dumps):
        dout("dump_" + nm, dshapes[nm], BF16 if nm in ("qk", "av") else F32)
    dumps = set(dumps)

    es = contextlib.ExitStack()
    with es:
        SB = lambda name, shape, dt=F32: es.enter_context(nc.sbuf_tensor(name, shape, dt))
        PS = lambda name, shape, dt=F32: es.enter_context(nc.psum_tensor(name, shape, dt))
        P = Prog(nc, es)

        onesb = SB("onesb", [1, 128], BF16)
        onescb = SB("onescb", [128, 1], BF16)
        ones2c = SB("ones2c", [128, 2], BF16)
        ones2r = SB("ones2r", [2, 128], BF16)
        idb = SB("idb", [128, 128], BF16)
        idf = SB("idf", [128, 128])
        rowstage = SB("rowstage", [64, 128])
        pmask_sb = SB("pmask_sb", [128, 4])
        ccol = SB("ccol", [128, 2])
        pse = None

        dm = {}

        mes = contextlib.ExitStack()
        with mes:
            MSB = lambda name, shape, dt=F32: mes.enter_context(nc.sbuf_tensor(name, shape, dt))
            woutb = MSB("woutb", [128, PGM * DM], BF16)
            cols = MSB("cols", [128, 48])
            C_WDT, C_BDT, C_CW, C_CB, C_D, C_LNW, C_LNB = 0, 4, 8, 24, 28, 32, 36
            arep = MSB("arep", [128, DS])
            carry = MSB("carry", [128, PGM * DS])
            xi_slab = MSB("xi_slab", [128, PGM * T])
            pse = mes.enter_context(nc.psum_tensor("pse", [128, 256], F32))
            pse2 = mes.enter_context(nc.psum_tensor("pse2", [128, DS], F32))
            w_es = contextlib.ExitStack()
            winb = w_es.enter_context(nc.sbuf_tensor("winb", [128, 8 * DM], BF16))
            wxb = w_es.enter_context(nc.sbuf_tensor("wxb", [128, PGM * 33], BF16))
            setup_es = contextlib.ExitStack()
            winstage = setup_es.enter_context(nc.sbuf_tensor("winstage", [128, 8 * DM], F32))
            acol = setup_es.enter_context(nc.sbuf_tensor("acol", [1, DS], F32))
            ahl = setup_es.enter_context(nc.sbuf_tensor("ahl", [1, 2 * DS], BF16))
            arow = setup_es.enter_context(nc.sbuf_tensor("arow", [1, 2 * DS], F32))

            dm["winT"] = P.op("sync", lambda h: h.dma_start(
                out=winstage[:].rearrange("p (k j) -> p k j", k=8),
                in_=A["winT"].rearrange("(k p) j -> p k j", p=128)), dma=True)
            dm["idf"] = P.op("sync", lambda h: h.dma_start(out=idf[:], in_=A["ident"]), dma=True)
            dm["idb"] = P.op("sync", lambda h: h.dma_start(out=idb[:], in_=A["identb"]), dma=True)
            dm["alog"] = P.op("sync", lambda h: h.dma_start(out=acol[:], in_=A["alog"]), dma=True)
            dm["pm"] = P.op("sync", lambda h: h.dma_start(out=pmask_sb[:], in_=A["pmask"]), dma=True)

            P.op("vector", lambda h: h.memset(ccol[:, 0:1], 1e-5))
            P.op("vector", lambda h: h.memset(ccol[:, 1:2], 1.0))
            P.op("vector", lambda h: h.memset(onesb[:], 1.0))
            P.op("vector", lambda h: h.memset(onescb[:], 1.0))
            P.op("vector", lambda h: h.memset(carry[:], 0.0))
            P.op("sync", lambda h: h.dma_start(out=ones2c[:], in_=A["ones2c_in"]), dma=True)
            P.op("sync", lambda h: h.dma_start(out=ones2r[:], in_=A["ones2r_in"]), dma=True)
            ev_ones = P.op("vector", lambda h: h.memset(onesb[:], 1.0))

            e1 = P.op("vector", lambda h: h.tensor_scalar(
                out=acol[:], in0=acol[:], scalar1=5.0, scalar2=None, op0=ALU.min),
                wait=[dm["alog"]])
            e2 = P.op("scalar", lambda h: h.activation(out=acol[:], in_=acol[:], func=AF.Exp),
                      wait=[e1])
            e3 = P.op("vector", lambda h: h.tensor_scalar(
                out=acol[:], in0=acol[:], scalar1=-1.0, scalar2=None, op0=ALU.mult), wait=[e2])
            e4 = P.op("vector", lambda h: h.tensor_copy(out=ahl[:, 0:DS], in_=acol[:]), wait=[e3])
            e5 = P.op("scalar", lambda h: h.activation(out=arow[:, 0:DS], in_=ahl[:, 0:DS],
                                                       func=AF.Copy), wait=[e4])
            e6 = P.op("vector", lambda h: h.tensor_tensor(out=arow[:, DS:2 * DS], in0=acol[:],
                                                          in1=arow[:, 0:DS], op=ALU.subtract),
                      wait=[e5])
            e7 = P.op("vector", lambda h: h.tensor_copy(out=ahl[:, DS:2 * DS],
                                                        in_=arow[:, DS:2 * DS]), wait=[e6])
            e8 = P.op("tensor", lambda h: h.matmul(out=pse2[:], lhsT=onesb[:], rhs=ahl[:, 0:DS],
                                                   start=True, stop=False), wait=[e7, ev_ones])
            e9 = P.op("tensor", lambda h: h.matmul(out=pse2[:], lhsT=onesb[:],
                                                   rhs=ahl[:, DS:2 * DS], start=False, stop=True))
            ev_arep = P.op("vector", lambda h: h.tensor_copy(out=arep[:], in_=pse2[:]), wait=[e9])

            rown = [("wdt_r", C_WDT, PGM), ("bdt_r", C_BDT, PGM), ("convw_r", C_CW, 16),
                    ("convb_r", C_CB, PGM), ("d_r", C_D, PGM), ("lnw_r", C_LNW, PGM),
                    ("lnb_r", C_LNB, PGM)]
            prev_tr, prev_ev = ev_arep, ev_arep
            for nm, off, n in rown:
                ldr = P.op("sync", lambda h, nm=nm, n=n: h.dma_start(
                    out=rowstage[0:n, 0:128], in_=A[nm]), wait=[prev_tr], dma=True)
                tr = P.op("tensor", lambda h, n=n: h.transpose(
                    out=pse[:, 0:n], in_=rowstage[0:n, 0:128], identity=idf[0:n, 0:n]),
                    wait=[ldr, prev_ev, dm["idf"]])
                evc = P.op("vector", lambda h, off=off, n=n: h.tensor_copy(
                    out=cols[:, off:off + n], in_=pse[:, 0:n]), wait=[tr])
                prev_tr, prev_ev = tr, evc

            cw = []
            for k in range(8):
                cw.append(P.op("scalar", lambda h, k=k: h.activation(
                    out=winb[:, k * DM:(k + 1) * DM], in_=winstage[:, k * DM:(k + 1) * DM],
                    func=AF.Copy), wait=[dm["winT"]] if k == 0 else []))
            ld2 = P.op("sync", lambda h: h.dma_start(
                out=winstage[:, 0:PGM * DM].rearrange("p (k j) -> p k j", k=PGM),
                in_=A["woutT"].rearrange("(k p) j -> p k j", p=128)), wait=[cw[-1]], dma=True)
            cw2 = []
            for k in range(PGM):
                cw2.append(P.op("scalar", lambda h, k=k: h.activation(
                    out=woutb[:, k * DM:(k + 1) * DM], in_=winstage[:, k * DM:(k + 1) * DM],
                    func=AF.Copy), wait=[ld2] if k == 0 else []))
            ld3 = P.op("sync", lambda h: h.dma_start(
                out=winstage[:, 0:PGM * 33].rearrange("p (k j) -> p k j", k=PGM),
                in_=A["wxT"].rearrange("(k p) j -> p k j", p=128)), wait=[cw2[-1]], dma=True)
            P.op("scalar", lambda h: h.activation(out=wxb[:], in_=winstage[:, 0:PGM * 33],
                                                  func=AF.Copy), wait=[ld3])

            P.fence()
            setup_es.close()
            if lim == 0:
                w_es.close()
                P.flush()
                return nc, A

            m1s = contextlib.ExitStack()
            with m1s:
                SB1 = lambda name, shape, dt=F32: m1s.enter_context(nc.sbuf_tensor(name, shape, dt))
                PS1 = lambda name, shape, dt=F32: m1s.enter_context(nc.psum_tensor(name, shape, dt))
                xbuf = SB1("xbuf", [128, 2 * 8 * TT1])
                xwb = SB1("xwb", [128, 8 * TT1], BF16)
                xwsq = SB1("xwsq", [128, 8 * TT1], BF16)
                xnb = SB1("xnb", [128, 8 * TT1], BF16)
                xc0 = SB1("xc0", [128, PGM * (TT1 + 3)])
                accb = SB1("accb", [128, TT1])
                sub8 = SB1("sub8", [128, TT1])
                zstg = SB1("zstg", [128, TT1])
                xistb = SB1("xistb", [128, PGM * TT1], BF16)
                xdst = SB1("xdst", [33, TT1])
                m1row = SB1("m1row", [1, 4 * TT1])
                m1rb = SB1("m1rb", [1, 4 * TT1], BF16)
                p_st = PS1("p_st", [1, TT1])
                p_bc = PS1("p_bc", [128, 1024])
                p_xz0 = PS1("p_xz0", [128, TT1])
                p_xz1 = PS1("p_xz1", [128, TT1])
                p_xd = PS1("p_xd", [33, TT1])

                ev_zero = P.op("vector", lambda h: h.memset(xc0[:], 0.0))
                n1 = m1tiles if m1tiles else T // TT1
                for g in range(n1):
                    t0 = g * TT1
                    xo = (g % 2) * 8 * TT1
                    w = [dm["xn", g - 2]] if g >= 2 else []
                    dm["x", g] = P.op("sync", lambda h, t0=t0, xo=xo: h.dma_start(
                        out=xbuf[:, xo:xo + 8 * TT1].rearrange("p (pg t) -> p pg t", pg=8),
                        in_=A["xT"][:, t0:t0 + TT1].rearrange("(pg p) t -> p pg t", p=128)),
                        wait=w, dma=True)

                    sqw = [dm["x", g]] + ([dm["statmm", g - 1]] if g >= 1 else [])
                    for pg in range(8):
                        xap = xbuf[:, xo + pg * TT1: xo + (pg + 1) * TT1]
                        P.op("scalar", lambda h, xap=xap, pg=pg: h.activation(
                            out=xwb[:, pg * TT1:(pg + 1) * TT1], in_=xap, func=AF.Copy),
                            wait=sqw if pg == 0 else [])
                        P.op("scalar", lambda h, xap=xap, pg=pg: h.activation(
                            out=xwsq[:, pg * TT1:(pg + 1) * TT1], in_=xap, func=AF.Square))
                    dm["sq", g] = P.mark("scalar")

                    def statmm(h):
                        for pg in range(8):
                            h.matmul(out=p_st[:, 0:TT1], lhsT=onescb[:],
                                     rhs=xwb[:, pg * TT1:(pg + 1) * TT1],
                                     start=(pg == 0), stop=(pg == 7))
                        for pg in range(8):
                            ins = h.matmul(out=pse[0:1, 0:TT1], lhsT=onescb[:],
                                           rhs=xwsq[:, pg * TT1:(pg + 1) * TT1],
                                           start=(pg == 0), stop=(pg == 7))
                        return ins
                    sw = [dm["sq", g]] + ([dm["murows", g - 1]] if g >= 1 else [])
                    dm["statmm", g] = P.op("tensor", statmm, wait=sw, n=16)

                    P.op("vector", lambda h: h.tensor_scalar(
                        out=m1row[:, 0:TT1], in0=p_st[:],
                        scalar1=1.0 / DM, scalar2=None, op0=ALU.mult), wait=[dm["statmm", g]])
                    P.op("vector", lambda h: h.tensor_scalar(
                        out=m1row[:, TT1:2 * TT1], in0=pse[0:1, 0:TT1],
                        scalar1=1.0 / DM, scalar2=None, op0=ALU.mult))
                    P.op("vector", lambda h: h.tensor_tensor(
                        out=m1row[:, 2 * TT1:3 * TT1], in0=m1row[:, 0:TT1],
                        in1=m1row[:, 0:TT1], op=ALU.mult))
                    ev1 = P.op("vector", lambda h: h.tensor_tensor(
                        out=m1row[:, TT1:2 * TT1], in0=m1row[:, TT1:2 * TT1],
                        in1=m1row[:, 2 * TT1:3 * TT1], op=ALU.subtract))
                    ev2 = P.op("scalar", lambda h: h.activation(
                        out=m1row[:, TT1:2 * TT1], in_=m1row[:, TT1:2 * TT1],
                        func=AF.Ln, bias=ccol[0:1, 0:1]), wait=[ev1])
                    ev3 = P.op("scalar", lambda h: h.activation(
                        out=m1row[:, TT1:2 * TT1], in_=m1row[:, TT1:2 * TT1],
                        func=AF.Exp, scale=-0.5))
                    ev4 = P.op("vector", lambda h: h.tensor_copy(
                        out=m1rb[:, 0:2 * TT1], in_=m1row[:, 0:2 * TT1]), wait=[ev3])
                    ev5 = P.op("scalar", lambda h: h.activation(
                        out=m1row[:, 2 * TT1:4 * TT1], in_=m1rb[:, 0:2 * TT1], func=AF.Copy),
                        wait=[ev4])
                    ev6 = P.op("vector", lambda h: h.tensor_tensor(
                        out=m1row[:, 2 * TT1:4 * TT1], in0=m1row[:, 0:2 * TT1],
                        in1=m1row[:, 2 * TT1:4 * TT1], op=ALU.subtract), wait=[ev5])
                    evm = P.op("vector", lambda h: h.tensor_copy(
                        out=m1rb[:, 2 * TT1:4 * TT1], in_=m1row[:, 2 * TT1:4 * TT1]), wait=[ev6])
                    dm["murows", g] = evm

                    def bcast(h):
                        h.matmul(out=p_bc[:, 0:TT1], lhsT=onesb[:], rhs=m1rb[:, 0:TT1],
                                 start=True, stop=False)
                        h.matmul(out=p_bc[:, 0:TT1], lhsT=onesb[:], rhs=m1rb[:, 2 * TT1:3 * TT1],
                                 start=False, stop=True)
                        h.matmul(out=p_bc[:, 512:512 + TT1], lhsT=onesb[:], rhs=m1rb[:, TT1:2 * TT1],
                                 start=True, stop=False)
                        return h.matmul(out=p_bc[:, 512:512 + TT1], lhsT=onesb[:],
                                        rhs=m1rb[:, 3 * TT1:4 * TT1], start=False, stop=True)
                    bw = [evm] + ([dm["xn", g - 1]] if g >= 1 else [])
                    evbc = P.op("tensor", bcast, wait=bw, n=4)

                    for pg in range(8):
                        xap = xbuf[:, xo + pg * TT1: xo + (pg + 1) * TT1]
                        P.op("vector", lambda h, xap=xap: h.tensor_tensor(
                            out=sub8[:], in0=xap, in1=p_bc[:, 0:TT1], op=ALU.subtract),
                            wait=[evbc] if pg == 0 else [])
                        ev_ = P.op("vector", lambda h, pg=pg: h.tensor_tensor(
                            out=xnb[:, pg * TT1:(pg + 1) * TT1], in0=sub8[:],
                            in1=p_bc[:, 512:512 + TT1], op=ALU.mult))
                    dm["xn", g] = ev_

                    for j in range(8):
                        pxz = p_xz0 if (j % 2) == 0 else p_xz1

                        def xzmm(h, j=j, pxz=pxz):
                            for k in range(8):
                                ins = h.matmul(
                                    out=pxz[:],
                                    lhsT=winb[:, k * DM + j * 128: k * DM + (j + 1) * 128],
                                    rhs=xnb[:, k * TT1:(k + 1) * TT1],
                                    start=(k == 0), stop=(k == 7))
                            return ins
                        mmw = [dm["xn", g]]
                        if j >= 2:
                            mmw.append(dm["xzdone", g, j - 2])
                        elif g >= 1:
                            mmw.append(dm["xzdone", g - 1, j + 6])
                        dm["xzmm", g, j] = P.op("tensor", xzmm, wait=mmw, n=8)

                        if j < PGM:
                            co = j * (TT1 + 3)
                            evw = [dm["xzmm", g, j], ev_zero]
                            if g >= 1:
                                evw.append(dm["tail", g - 1, j])
                            eva = P.op("scalar", lambda h, co=co, pxz=pxz: h.activation(
                                out=xc0[:, co + 3:co + 3 + TT1], in_=pxz[:],
                                func=AF.Copy), wait=evw)

                            P.op("vector", lambda h, co=co, j=j: h.tensor_scalar(
                                out=accb[:], in0=xc0[:, co + 3:co + 3 + TT1],
                                scalar1=cols[:, C_CW + 12 + j:C_CW + 13 + j],
                                scalar2=None, op0=ALU.mult), wait=[eva])
                            for k in range(3):
                                evc = P.op("vector", lambda h, co=co, j=j, k=k: h.scalar_tensor_tensor(
                                    out=accb[:], in0=xc0[:, co + k:co + k + TT1],
                                    scalar=cols[:, C_CW + 4 * k + j:C_CW + 4 * k + j + 1],
                                    in1=accb[:], op0=ALU.mult, op1=ALU.add))
                            dm["tail", g, j] = P.op("vector", lambda h, co=co: h.tensor_copy(
                                out=xc0[:, co:co + 3], in_=xc0[:, co + TT1:co + TT1 + 3]))
                            P.op("scalar", lambda h, j=j, t0=t0: h.activation(
                                out=xi_slab[:, j * T + t0: j * T + t0 + TT1], in_=accb[:],
                                func=AF.Silu, bias=cols[:, C_CB + j:C_CB + j + 1]), wait=[evc])
                            dm["xzdone", g, j] = P.op("scalar", lambda h, j=j: h.activation(
                                out=xistb[:, j * TT1:(j + 1) * TT1], in_=accb[:],
                                func=AF.Silu, bias=cols[:, C_CB + j:C_CB + j + 1]))
                        else:
                            zw = [dm["xzmm", g, j]] + ([dm["zlast"]] if "zlast" in dm else [])
                            evz = P.op("scalar", lambda h, pxz=pxz: h.activation(
                                out=zstg[:], in_=pxz[:], func=AF.Silu), wait=zw)
                            dm["xzdone", g, j] = evz
                            dm["zlast"] = P.op("gpsimd", lambda h, j=j, t0=t0: h.dma_start(
                                out=A["b_zs"][(j - PGM) * 128:(j - PGM + 1) * 128, t0:t0 + TT1],
                                in_=zstg[:]), wait=[evz], dma=True)
                            dm["zdma", g, j] = dm["zlast"]

                    def xdmm(h):
                        for k in range(PGM):
                            ins = h.matmul(out=p_xd[:], lhsT=wxb[:, k * 33:(k + 1) * 33],
                                           rhs=xistb[:, k * TT1:(k + 1) * TT1],
                                           start=(k == 0), stop=(k == PGM - 1))
                        return ins
                    xdw = [dm["xzdone", g, j] for j in range(PGM)]
                    if g >= 1:
                        xdw.append(dm["xdev", g - 1])
                    evxd = P.op("tensor", xdmm, wait=xdw, n=PGM)
                    xew = [evxd] + ([dm["xddma", g - 1]] if g >= 1 else [])
                    dm["xdev", g] = P.op("scalar", lambda h: h.activation(
                        out=xdst[:], in_=p_xd[:], func=AF.Copy), wait=xew)
                    dm["xddma", g] = P.op("gpsimd", lambda h, t0=t0: h.dma_start(
                        out=A["b_xd_in"][:, t0:t0 + TT1], in_=xdst[:]),
                        wait=[dm["xdev", g]], dma=True)

                if nocc:
                    m1s.close()
                    w_es.close()
                    P.flush()
                    return nc, A
                ccxd = P.op("gpsimd", lambda h: h.collective_compute(
                    "AllReduce", ALU.add, replica_groups=GROUPS,
                    ins=[A["b_xd_in"]], outs=[A["b_xd"]]),
                    wait=[dm["xddma", n1 - 1], dm["xddma", n1 - 2],
                          dm["zdma", n1 - 1, 7], dm["zdma", n1 - 1, 6]], cc=True)

                hs1 = SB1("hs1", [33, 512])
                hs2 = SB1("hs2", [33, 512])
                hsb = SB1("hsb", [33, 2 * 512], BF16)
                prevd = ccxd
                prevd1 = ccxd
                for i in range(T // 512):
                    o = i * 512
                    eL = P.op("sync", lambda h, o=o: h.dma_start(
                        out=hs1[:], in_=A["b_xd"][:, o:o + 512]), wait=[prevd], dma=True)
                    c1 = P.op("vector", lambda h: h.tensor_copy(out=hsb[:, 0:512], in_=hs1[:]),
                              wait=[eL, prevd1])
                    c2 = P.op("scalar", lambda h: h.activation(out=hs2[:], in_=hsb[:, 0:512],
                                                               func=AF.Copy), wait=[c1])
                    c3 = P.op("vector", lambda h: h.tensor_tensor(
                        out=hs2[:], in0=hs1[:], in1=hs2[:], op=ALU.subtract), wait=[c2])
                    c4 = P.op("vector", lambda h: h.tensor_copy(out=hsb[:, 512:1024], in_=hs2[:]),
                              wait=[c3])
                    d1 = P.op("gpsimd", lambda h, o=o: h.dma_start(
                        out=A["b_xdh"][:, o:o + 512], in_=hsb[:, 0:512]), wait=[c1], dma=True)
                    prevd1 = d1
                    d2 = P.op("gpsimd", lambda h, o=o: h.dma_start(
                        out=A["b_xdl"][:, o:o + 512], in_=hsb[:, 512:1024]), wait=[c4], dma=True)
                    prevd = d2
                if "xd" in dumps:
                    P.op("gpsimd", lambda h: h.dma_start(
                        out=A["dump_xd"], in_=A["b_xd"]), wait=[prevd], dma=True)
                if "xi" in dumps:
                    for pg in range(PGM):
                        P.op("gpsimd", lambda h, pg=pg: h.dma_start(
                            out=A["dump_xi"][pg * 128:(pg + 1) * 128, :],
                            in_=xi_slab[:, pg * T:(pg + 1) * T]), wait=[prevd], dma=True)
                if "zs" in dumps:
                    P.op("gpsimd", lambda h: h.dma_start(
                        out=A["dump_zs"], in_=A["b_zs"]), wait=[dm["zdma", n1 - 1, 7]], dma=True)

            P.fence()
            w_es.close()
            if lim == 1:
                P.flush()
                return nc, A

            m2s = contextlib.ExitStack()
            with m2s:
                SB2 = lambda name, shape, dt=F32: m2s.enter_context(nc.sbuf_tensor(name, shape, dt))
                PS2 = lambda name, shape, dt=F32: m2s.enter_context(nc.psum_tensor(name, shape, dt))
                bz = SB2("bz", [128, DS * TT2])
                bda = SB2("bda", [128, DS * TT2])
                bu = SB2("bu", [128, DS * TT2])
                bh = SB2("bh", [128, DS * TT2])
                brep = SB2("brep", [128, DS * TT2])
                crep = SB2("crep", [128, DS * TT2])
                bcfl = SB2("bcfl", [1, 2 * DS * TT2], BF16)
                dtr = SB2("dtr", [1, 2 * TT2], BF16)
                argb = SB2("argb", [128, TT2])
                absb = SB2("absb", [128, TT2])
                relu_b = SB2("relu_b", [128, TT2])
                dtloc = SB2("dtloc", [128, TT2])
                dtps = SB2("dtps", [128, TT2])
                gloc = SB2("gloc", [128, TT2])
                ybuf = SB2("ybuf", [128, TT2])
                ybb = SB2("ybb", [128, 2 * TT2], BF16)
                ystg = SB2("ystg", [1, 2 * TT2])
                p_rep = PS2("p_rep", [128, DS * TT2])
                p_dt = PS2("p_dt", [128, TT2])
                p_ys = PS2("p_ys", [1, TT2])

                n2 = m2tiles if m2tiles else T // TT2
                SW = DS * TT2
                for g in range(n2):
                    t0 = g * TT2
                    dw = [dm["mmC", g - 1], dm["evdt", g - 1]] if g >= 1 else []
                    dm["bfl", g] = P.op("sync", lambda h, t0=t0: h.dma_start(
                        out=bcfl[:, 0:SW].rearrange("p (s t) -> p s t", s=DS),
                        in_=A["b_xdh"][1:17, t0:t0 + TT2].unsqueeze(0)),
                        wait=dw, dma=True)
                    dm["bfl2", g] = P.op("sync", lambda h, t0=t0: h.dma_start(
                        out=bcfl[:, SW:2 * SW].rearrange("p (s t) -> p s t", s=DS),
                        in_=A["b_xdl"][1:17, t0:t0 + TT2].unsqueeze(0)), dma=True)
                    dm["dtrh", g] = P.op("sync", lambda h, t0=t0: h.dma_start(
                        out=dtr[:, 0:TT2], in_=A["b_xdh"][0:1, t0:t0 + TT2]), dma=True)
                    dm["dtr", g] = P.op("sync", lambda h, t0=t0: h.dma_start(
                        out=dtr[:, TT2:2 * TT2], in_=A["b_xdl"][0:1, t0:t0 + TT2]), dma=True)

                    def repmm(h):
                        for c in range(4):
                            sl = slice(c * 512, (c + 1) * 512)
                            h.matmul(out=p_rep[:, sl], lhsT=onesb[:],
                                     rhs=bcfl[:, c * 512:(c + 1) * 512],
                                     start=True, stop=False)
                            ins = h.matmul(out=p_rep[:, sl], lhsT=onesb[:],
                                           rhs=bcfl[:, SW + c * 512: SW + (c + 1) * 512],
                                           start=False, stop=True)
                        return ins
                    bw = [dm["bfl", g], dm["bfl2", g]] + ([dm["brepe", g - 1]] if g >= 1 else [])
                    mmB = P.op("tensor", repmm, wait=bw, n=8)
                    dm["mmB", g] = mmB
                    dm["brepe", g] = P.op("scalar", lambda h: h.activation(
                        out=brep[:], in_=p_rep[:], func=AF.Copy),
                        wait=[mmB] + ([dm["um", g - 1]] if g >= 1 else []))
                    cf1 = P.op("sync", lambda h, t0=t0: h.dma_start(
                        out=bcfl[:, 0:SW].rearrange("p (s t) -> p s t", s=DS),
                        in_=A["b_xdh"][17:33, t0:t0 + TT2].unsqueeze(0)),
                        wait=[mmB], dma=True)
                    cf2 = P.op("sync", lambda h, t0=t0: h.dma_start(
                        out=bcfl[:, SW:2 * SW].rearrange("p (s t) -> p s t", s=DS),
                        in_=A["b_xdl"][17:33, t0:t0 + TT2].unsqueeze(0)), dma=True)
                    mmC = P.op("tensor", repmm, wait=[cf1, cf2, dm["brepe", g]], n=8)
                    dm["mmC", g] = mmC
                    dm["crepe", g] = P.op("scalar", lambda h: h.activation(
                        out=crep[:], in_=p_rep[:], func=AF.Copy),
                        wait=[mmC] + ([dm["ym", g - 1]] if g >= 1 else []))

                    def dtmm(h):
                        h.matmul(out=p_dt[:], lhsT=onesb[:], rhs=dtr[:, 0:TT2],
                                 start=True, stop=False)
                        return h.matmul(out=p_dt[:], lhsT=onesb[:], rhs=dtr[:, TT2:2 * TT2],
                                        start=False, stop=True)
                    dtw = [dm["dtr", g], dm["dtrh", g]] + ([dm["dtc", g - 1, PGM - 1]] if g >= 1 else [])
                    evdt0 = P.op("tensor", dtmm, wait=dtw, n=2)
                    evdt = P.op("scalar", lambda h: h.activation(
                        out=dtps[:], in_=p_dt[:], func=AF.Copy),
                        wait=[evdt0] + ([dm["dtc", g - 1, PGM - 1]] if g >= 1 else []))
                    dm["evdt", g] = evdt

                    for pg in range(PGM):
                        a1w = [evdt]
                        if pg >= 1:
                            a1w = []
                        dm["dtc", g, pg] = a1 = P.op("vector", lambda h, pg=pg: h.scalar_tensor_tensor(
                            out=argb[:], in0=dtps[:],
                            scalar=cols[:, C_WDT + pg:C_WDT + pg + 1],
                            in1=cols[:, C_BDT + pg:C_BDT + pg + 1].broadcast_to([128, TT2]),
                            op0=ALU.mult, op1=ALU.add), wait=a1w + ([dm["spdone", g, pg - 1]] if pg >= 1 else ([dm["spdone", g - 1, PGM - 1]] if g >= 1 else [])))
                        a2 = P.op("scalar", lambda h: h.activation(
                            out=absb[:], in_=argb[:], func=AF.Abs), wait=[a1])
                        a3 = P.op("scalar", lambda h: h.activation(
                            out=absb[:], in_=absb[:], func=AF.Exp, scale=-1.0), wait=[a2])
                        a4 = P.op("scalar", lambda h: h.activation(
                            out=absb[:], in_=absb[:], func=AF.Ln, bias=ccol[:, 1:2]))
                        a5 = P.op("vector", lambda h: h.tensor_scalar(
                            out=relu_b[:], in0=argb[:], scalar1=0.0, scalar2=None,
                            op0=ALU.max), wait=[a4])
                        a6 = P.op("vector", lambda h: h.tensor_tensor(
                            out=dtloc[:], in0=absb[:], in1=relu_b[:], op=ALU.add))
                        dm["spdone", g, pg] = a7 = P.op("vector", lambda h: h.tensor_scalar(
                            out=dtloc[:], in0=dtloc[:], scalar1=1e-4, scalar2=10.0,
                            op0=ALU.max, op1=ALU.min))
                        gw = [a7] + ([dm["umul", g, pg - 1]] if pg >= 1 else ([dm["umul", g - 1, PGM - 1]] if g >= 1 else []))
                        g1 = P.op("vector", lambda h, pg=pg, t0=t0: h.tensor_tensor(
                            out=gloc[:], in0=dtloc[:],
                            in1=xi_slab[:, pg * T + t0: pg * T + t0 + TT2], op=ALU.mult), wait=gw)
                        zw = [a7] + ([dm["damax", g, pg - 1]] if pg >= 1 else ([dm["damax", g - 1, PGM - 1]] if g >= 1 else []))
                        z1 = P.op("vector", lambda h: h.tensor_tensor(
                            out=bz[:].rearrange("p (s t) -> p s t", s=DS),
                            in0=dtloc[:].unsqueeze(1).broadcast_to([128, DS, TT2]),
                            in1=arep[:].unsqueeze(2).broadcast_to([128, DS, TT2]),
                            op=ALU.mult), wait=zw)
                        xw = [z1] + ([dm["uclip", g, pg - 1]] if pg >= 1 else ([dm["uclip", g - 1, PGM - 1]] if g >= 1 else []))
                        x1e = P.op("scalar", lambda h: h.activation(
                            out=bda[:], in_=bz[:], func=AF.Exp), wait=xw)
                        pw = [x1e] + ([dm["scan", g, pg - 1]] if pg >= 1 else ([dm["scan", g - 1, PGM - 1]] if g >= 1 else []))
                        dm["damax", g, pg] = p1 = P.op("gpsimd", lambda h: h.tensor_scalar(
                            out=bz[:], in0=bda[:], scalar1=3.4e38, scalar2=EXP20,
                            op0=ALU.min, op1=ALU.max), wait=pw)
                        uw = [g1, p1, dm["brepe", g]]
                        dm["umul", g, pg] = u1 = P.op("vector", lambda h: h.tensor_tensor(
                            out=bda[:].rearrange("p (s t) -> p s t", s=DS),
                            in0=gloc[:].unsqueeze(1).broadcast_to([128, DS, TT2]),
                            in1=brep[:].rearrange("p (s t) -> p s t", s=DS),
                            op=ALU.mult), wait=uw)
                        cwt = [u1] + ([dm["scan", g, pg - 1]] if pg >= 1 else ([dm["scan", g - 1, PGM - 1]] if g >= 1 else []))
                        p2 = P.op("gpsimd", lambda h: h.tensor_scalar(
                            out=bu[:], in0=bda[:], scalar1=10.0, scalar2=-10.0,
                            op0=ALU.min, op1=ALU.max), wait=cwt)
                        dm["uclip", g, pg] = p2

                        def scan(h, pg=pg, g=g):
                            for s in range(DS):
                                init = 0.0 if g == 0 else carry[:, pg * DS + s: pg * DS + s + 1]
                                ins = h.tensor_tensor_scan(
                                    out=bh[:, s * TT2:(s + 1) * TT2],
                                    data0=bz[:, s * TT2:(s + 1) * TT2],
                                    data1=bu[:, s * TT2:(s + 1) * TT2],
                                    initial=init, op0=ALU.mult, op1=ALU.add)
                            return ins
                        dm["scan", g, pg] = sc = P.op("vector", scan, wait=[p1, p2])
                        ymw = [sc, dm["crepe", g]]
                        ym = P.op("vector", lambda h: h.tensor_tensor(
                            out=bda[:].rearrange("p (t s) -> p t s", s=DS),
                            in0=bh[:].rearrange("p (s t) -> p s t", s=DS).transpose([0, 2, 1]),
                            in1=crep[:].rearrange("p (s t) -> p s t", s=DS).transpose([0, 2, 1]),
                            op=ALU.mult), wait=ymw)
                        dm["ym", g] = ym
                        P.op("vector", lambda h, pg=pg: h.tensor_copy(
                            out=carry[:, pg * DS:(pg + 1) * DS],
                            in_=bh[:].rearrange("p (s t) -> p s t", s=DS)[:, :, TT2 - 1:TT2].squeeze(2)))
                        yr = P.op("vector", lambda h: h.tensor_reduce(
                            out=ybuf[:], in_=bda[:].rearrange("p (t s) -> p t s", s=DS),
                            axis=AX.X, op=ALU.add),
                            wait=[dm["ylast"]] if "ylast" in dm else [])
                        dm["um", g] = yr
                        ycw = [yr] + ([dm["yslast"]] if "yslast" in dm else [])
                        yc = P.op("scalar", lambda h: h.activation(
                            out=ybb[:, 0:TT2], in_=ybuf[:], func=AF.Copy), wait=ycw)
                        ys = P.op("scalar", lambda h: h.activation(
                            out=ybb[:, TT2:2 * TT2], in_=ybuf[:], func=AF.Square))

                        def ysmm(h, pg=pg):
                            h.matmul(out=p_ys[:], lhsT=onescb[:], rhs=ybb[:, 0:TT2],
                                     start=(pg == 0), stop=(pg == PGM - 1))
                            return h.matmul(out=pse[0:1, 0:TT2], lhsT=onescb[:],
                                            rhs=ybb[:, TT2:2 * TT2],
                                            start=(pg == 0), stop=(pg == PGM - 1))
                        ysw = [ys] + ([dm["ystev", g - 1]] if g >= 1 and pg == 0 else [])
                        dm["ysmm", g, pg] = P.op("tensor", ysmm, wait=ysw, n=2)
                        dm["yslast"] = dm["ysmm", g, pg]
                        dm["ylast"] = P.op("gpsimd", lambda h, pg=pg, t0=t0: h.dma_start(
                            out=A["b_y"][pg * 128:(pg + 1) * 128, t0:t0 + TT2], in_=ybuf[:]),
                            wait=[yr], dma=True)
                        dm["ydma", g, pg] = dm["ylast"]

                    sew = [dm["ysmm", g, PGM - 1]] + ([dm["ystdma", g - 1]] if g >= 1 else [])
                    P.op("vector", lambda h: h.tensor_copy(
                        out=ystg[:, 0:TT2], in_=p_ys[:]), wait=sew)
                    dm["ystev", g] = P.op("vector", lambda h: h.tensor_copy(
                        out=ystg[:, TT2:2 * TT2], in_=pse[0:1, 0:TT2]))
                    dm["ystdma", g] = P.op("gpsimd", lambda h, t0=t0: h.dma_start(
                        out=A["b_st_in"][:, t0:t0 + TT2].unsqueeze(0),
                        in_=ystg[:].rearrange("p (a t) -> p a t", a=2)),
                        wait=[dm["ystev", g]], dma=True)

                ccst = P.op("gpsimd", lambda h: h.collective_compute(
                    "AllReduce", ALU.add, replica_groups=GROUPS,
                    ins=[A["b_st_in"]], outs=[A["b_st"]]),
                    wait=[dm["ystdma", n2 - 1], dm["ystdma", n2 - 2],
                          dm["ydma", n2 - 1, PGM - 1], dm["ydma", n2 - 1, PGM - 2]], cc=True)
                dm["ccst"] = ccst
                if "y" in dumps:
                    P.op("gpsimd", lambda h: h.dma_start(out=A["dump_y"], in_=A["b_y"]),
                         wait=[dm["ydma", n2 - 1, PGM - 1]], dma=True)
                if "stats" in dumps:
                    P.op("gpsimd", lambda h: h.dma_start(out=A["dump_stats"], in_=A["b_st"]),
                         wait=[ccst], dma=True)

            P.fence()
            if lim == 2:
                P.flush()
                return nc, A

            m3s = contextlib.ExitStack()
            with m3s:
                SB3 = lambda name, shape, dt=F32: m3s.enter_context(nc.sbuf_tensor(name, shape, dt))
                PS3 = lambda name, shape, dt=F32: m3s.enter_context(nc.psum_tensor(name, shape, dt))
                ytile = SB3("ytile", [128, 2 * TT3])
                zstile = SB3("zstile", [128, 2 * TT3])
                f1 = SB3("f1", [128, TT3])
                f2 = SB3("f2", [128, TT3])
                yhb = SB3("yhb", [128, PGM * TT3], BF16)
                strow = SB3("strow", [1, 2 * TT3])
                m3row = SB3("m3row", [1, 4 * TT3])
                m3rb = SB3("m3rb", [1, 4 * TT3], BF16)
                most = SB3("most", [128, 2 * TT3])
                p_mu = PS3("p_mu", [128, TT3])
                p_rs = PS3("p_rs", [128, TT3])
                p_mo = PS3("p_mo", [128, 2 * TT3])

                n3 = T // TT3
                for g in range(n3):
                    o = g * TT3
                    dw = [dm["m3row", g - 1]] if g >= 1 else [dm["ccst"]]
                    dm["st", g] = P.op("sync", lambda h, o=o: h.dma_start(
                        out=strow[:].rearrange("p (a t) -> p a t", a=2),
                        in_=A["b_st"][:, o:o + TT3].unsqueeze(0)), wait=dw, dma=True)

                    P.op("vector", lambda h: h.tensor_scalar(
                        out=m3row[:, 0:TT3], in0=strow[:, 0:TT3],
                        scalar1=1.0 / DI, scalar2=None, op0=ALU.mult), wait=[dm["st", g]])
                    P.op("vector", lambda h: h.tensor_scalar(
                        out=m3row[:, TT3:2 * TT3], in0=strow[:, TT3:2 * TT3],
                        scalar1=1.0 / DI, scalar2=None, op0=ALU.mult))
                    P.op("vector", lambda h: h.tensor_tensor(
                        out=m3row[:, 2 * TT3:3 * TT3], in0=m3row[:, 0:TT3],
                        in1=m3row[:, 0:TT3], op=ALU.mult))
                    r1 = P.op("vector", lambda h: h.tensor_tensor(
                        out=m3row[:, TT3:2 * TT3], in0=m3row[:, TT3:2 * TT3],
                        in1=m3row[:, 2 * TT3:3 * TT3], op=ALU.subtract))
                    r2 = P.op("scalar", lambda h: h.activation(
                        out=m3row[:, TT3:2 * TT3], in_=m3row[:, TT3:2 * TT3],
                        func=AF.Ln, bias=ccol[0:1, 0:1]), wait=[r1])
                    r3 = P.op("scalar", lambda h: h.activation(
                        out=m3row[:, TT3:2 * TT3], in_=m3row[:, TT3:2 * TT3],
                        func=AF.Exp, scale=-0.5))
                    r4 = P.op("vector", lambda h: h.tensor_copy(
                        out=m3rb[:, 0:2 * TT3], in_=m3row[:, 0:2 * TT3]), wait=[r3])
                    r5 = P.op("scalar", lambda h: h.activation(
                        out=m3row[:, 2 * TT3:4 * TT3], in_=m3rb[:, 0:2 * TT3], func=AF.Copy),
                        wait=[r4])
                    r6 = P.op("vector", lambda h: h.tensor_tensor(
                        out=m3row[:, 2 * TT3:4 * TT3], in0=m3row[:, 0:2 * TT3],
                        in1=m3row[:, 2 * TT3:4 * TT3], op=ALU.subtract), wait=[r5])
                    dm["m3row", g] = r7 = P.op("vector", lambda h: h.tensor_copy(
                        out=m3rb[:, 2 * TT3:4 * TT3], in_=m3row[:, 2 * TT3:4 * TT3]), wait=[r6])

                    def m3bc(h):
                        h.matmul(out=p_mu[:], lhsT=onesb[:], rhs=m3rb[:, 0:TT3],
                                 start=True, stop=False)
                        h.matmul(out=p_mu[:], lhsT=onesb[:], rhs=m3rb[:, 2 * TT3:3 * TT3],
                                 start=False, stop=True)
                        h.matmul(out=p_rs[:], lhsT=onesb[:], rhs=m3rb[:, TT3:2 * TT3],
                                 start=True, stop=False)
                        return h.matmul(out=p_rs[:], lhsT=onesb[:], rhs=m3rb[:, 3 * TT3:4 * TT3],
                                        start=False, stop=True)
                    bw = [r7] + ([dm["yh", g - 1, PGM - 1]] if g >= 1 else [])
                    evbc = P.op("tensor", m3bc, wait=bw, n=4)

                    par = (g % 2) * TT3
                    for pg in range(PGM):
                        uw = ([dm["yuse", g, pg - 1]] if pg >= 1 else
                              ([dm["yuse", g - 1, PGM - 1]] if g >= 1 else []))
                        yld = P.op("sync", lambda h, pg=pg, o=o, par=par: h.dma_start(
                            out=ytile[:, par:par + TT3],
                            in_=A["b_y"][pg * 128:(pg + 1) * 128, o:o + TT3]),
                            wait=uw, dma=True)
                        zld = P.op("sync", lambda h, pg=pg, o=o, par=par: h.dma_start(
                            out=zstile[:, par:par + TT3],
                            in_=A["b_zs"][pg * 128:(pg + 1) * 128, o:o + TT3]), dma=True)
                        w1 = [yld, evbc]
                        s1 = P.op("vector", lambda h, par=par: h.tensor_tensor(
                            out=f1[:], in0=ytile[:, par:par + TT3], in1=p_mu[:], op=ALU.subtract),
                            wait=w1)
                        s2 = P.op("vector", lambda h: h.tensor_tensor(
                            out=f2[:], in0=f1[:], in1=p_rs[:], op=ALU.mult))
                        s3 = P.op("vector", lambda h, pg=pg: h.scalar_tensor_tensor(
                            out=f1[:], in0=f2[:],
                            scalar=cols[:, C_LNW + pg:C_LNW + pg + 1],
                            in1=cols[:, C_LNB + pg:C_LNB + pg + 1].broadcast_to([128, TT3]),
                            op0=ALU.mult, op1=ALU.add))
                        s4 = P.op("vector", lambda h, pg=pg, o=o: h.scalar_tensor_tensor(
                            out=f2[:], in0=xi_slab[:, pg * T + o: pg * T + o + TT3],
                            scalar=cols[:, C_D + pg:C_D + pg + 1], in1=f1[:],
                            op0=ALU.mult, op1=ALU.add))
                        dm["yuse", g, pg] = s5 = P.op("vector", lambda h, pg=pg, par=par: h.tensor_tensor(
                            out=yhb[:, pg * TT3:(pg + 1) * TT3], in0=f2[:],
                            in1=zstile[:, par:par + TT3], op=ALU.mult), wait=[zld])
                        dm["yh", g, pg] = s5

                    for of in range(8):
                        po = (of % 2) * TT3

                        def momm(h, of=of, po=po):
                            for pg in range(PGM):
                                ins = h.matmul(
                                    out=p_mo[:, po:po + TT3],
                                    lhsT=woutb[:, pg * DM + of * 128: pg * DM + (of + 1) * 128],
                                    rhs=yhb[:, pg * TT3:(pg + 1) * TT3],
                                    start=(pg == 0), stop=(pg == PGM - 1))
                            return ins
                        mw = [dm["yh", g, PGM - 1]]
                        if of >= 2:
                            mw.append(dm["moev", g, of - 2])
                        elif g >= 1:
                            mw.append(dm["moev", g - 1, of + 6])
                        mo1 = P.op("tensor", momm, wait=mw, n=PGM)
                        ew = [mo1] + ([dm["molast", of % 2]] if ("molast", of % 2) in dm else [])
                        dm["moev", g, of] = P.op("scalar", lambda h, po=po, of=of: h.activation(
                            out=most[:, po:po + TT3], in_=p_mo[:, po:po + TT3], func=AF.Copy),
                            wait=ew)
                        dm["modma", g, of] = P.op("gpsimd", lambda h, of=of, o=o, po=po: h.dma_start(
                            out=A["b_mo_in"][o // QT, of * 128:(of + 1) * 128, (o % QT):(o % QT) + TT3],
                            in_=most[:, po:po + TT3]), wait=[dm["moev", g, of]], dma=True)
                        dm["molast", of % 2] = dm["modma", g, of]

                ccmo = P.op("gpsimd", lambda h: h.collective_compute(
                    "ReduceScatter", ALU.add, replica_groups=GROUPS,
                    ins=[A["b_mo_in"].rearrange("q f t -> (q f) t")], outs=[A["b_mo"]]),
                    wait=[dm["molast", 0], dm["molast", 1]], cc=True)
                dm["ccmo"] = ccmo
                if "mo" in dumps:
                    P.op("gpsimd", lambda h: h.dma_start(out=A["dump_mo"], in_=A["b_mo"]),
                         wait=[ccmo], dma=True)

            P.fence()

        if lim == 3:
            P.flush()
            return nc, A
        tes = contextlib.ExitStack()
        with tes:
            TSB = lambda name, shape, dt=F32: tes.enter_context(nc.sbuf_tensor(name, shape, dt))
            TPS = lambda name, shape, dt=F32: tes.enter_context(nc.psum_tensor(name, shape, dt))
            U = TSB("U", [128, 8 * TW])
            ub = TSB("ub", [128, 8 * TW], BF16)
            qh = TSB("qh", [128, TW], BF16)
            kh = TSB("kh", [128, TW], BF16)
            vh = TSB("vh", [128, TW], BF16)
            avb = TSB("avb", [128, 8 * QT], BF16)
            tabs = {nm: TSB("t_" + nm, [128, TW], BF16) for nm in ("cs", "ss", "ci", "si")}
            maskb = TSB("maskb", [128, 4 * 512], BF16)
            wst = TSB("wst", [128, 8 * 128])
            wstb = TSB("wstb", [128, 2 * 8 * 128], BF16)
            tcols = TSB("tcols", [128, 64])
            TC_QS, TC_KS, TC_G, TC_BLN, TC_B2, TC_B1A, TC_B1G = 0, 1, 2, 10, 18, 26, 48 - 22 + 22
            tcols2 = TSB("tcols2", [128, 80])
            x1t = ub[:, 0:8192].bitcast(F32)
            x1b = ub[:, 8192:12288]
            x1sq = ub[:, 12288:16384]
            trow = qh[0:1, 0:2048].bitcast(F32)
            trb = kh[0:1, 0:2048]
            srow = vh[0:1, 0:1024].bitcast(F32)
            qnf = TSB("qnf", [128, 512])
            tmpf = TSB("tmpf", [128, 512])
            s3f = TSB("s3f", [128, 512])
            sqb = TSB("sqb", [128, 512], BF16)
            ubt = TSB("ubt", [128, 512], BF16)
            nrow = TSB("nrow", [2, 2 * 512])
            nrb = TSB("nrb", [2, 2 * 512], BF16)
            negmax = TSB("negmax", [128, 1])
            lrow = TSB("lrow", [128, 1])
            rlcol = TSB("rlcol", [128, 1])
            Pb = s3f[:].bitcast(BF16)[:, 0:512]
            Pb2 = tmpf[:].bitcast(BF16)[:, 0:512]
            PTs = qnf[:].bitcast(BF16)[:, 0:512]
            VT = TSB("VT", [128, 2 * 18 * DH], BF16)
            gb_t = VT[:, 0:512]
            ab_t = VT[:, 512:1024]
            prodb = avb
            p_n = TPS("p_n", [2, 512])
            p_rb = TPS("p_rb", [128, 512])
            p_qkv = TPS("p_qkv", [128, 512])
            p_sim = TPS("p_sim", [128, 512])
            p_ptp = TPS("p_ptp", [128, 256], BF16)
            p_pt = p_ptp
            p_av = TPS("p_av", [128, 256])
            p_mo2 = TPS("p_mo2", [128, 512])

            for nm in ("cs", "ss", "ci", "si"):
                dm["tab", nm] = P.op("sync", lambda h, nm=nm: h.dma_start(
                    out=tabs[nm][:], in_=A[nm]), dma=True)
            dm["masks"] = P.op("sync", lambda h: h.dma_start(
                out=maskb[:].rearrange("p (m t) -> p m t", m=4),
                in_=A["masks"].rearrange("m p t -> p m t")), dma=True)

            ev_pm = dm["pm"]

            f1t = TSB("f1t", [128, TT3])
            f2t = VT[:, 1024:2048].bitcast(F32)

            def t_rows(sum_ap, sq_ap, denom, waits):
                v1 = P.op("vector", lambda h: h.tensor_scalar(
                    out=trow[:, 0:512], in0=sum_ap, scalar1=1.0 / denom, scalar2=None,
                    op0=ALU.mult), wait=waits)
                P.op("vector", lambda h: h.tensor_scalar(
                    out=trow[:, 512:1024], in0=sq_ap, scalar1=1.0 / denom, scalar2=None,
                    op0=ALU.mult))
                P.op("vector", lambda h: h.tensor_tensor(
                    out=srow[:], in0=trow[:, 0:512], in1=trow[:, 0:512], op=ALU.mult))
                v4 = P.op("vector", lambda h: h.tensor_tensor(
                    out=trow[:, 512:1024], in0=trow[:, 512:1024], in1=srow[:],
                    op=ALU.subtract))
                s1 = P.op("scalar", lambda h: h.activation(
                    out=trow[:, 512:1024], in_=trow[:, 512:1024], func=AF.Ln,
                    bias=ccol[0:1, 0:1]), wait=[v4])
                s2 = P.op("scalar", lambda h: h.activation(
                    out=trow[:, 512:1024], in_=trow[:, 512:1024], func=AF.Exp, scale=-0.5))
                last = s2
                for hseg in range(2):
                    sl = slice(hseg * 512, (hseg + 1) * 512)
                    c1 = P.op("vector", lambda h, sl=sl, hseg=hseg: h.tensor_copy(
                        out=trb[:, hseg * 512:(hseg + 1) * 512], in_=trow[:, sl]), wait=[last])
                    c2 = P.op("scalar", lambda h, hseg=hseg: h.activation(
                        out=srow[:], in_=trb[:, hseg * 512:(hseg + 1) * 512], func=AF.Copy),
                        wait=[c1])
                    c3 = P.op("vector", lambda h, sl=sl: h.tensor_tensor(
                        out=trow[:, sl], in0=trow[:, sl], in1=srow[:], op=ALU.subtract),
                        wait=[c2])
                    last = P.op("vector", lambda h, sl=sl, hseg=hseg: h.tensor_copy(
                        out=trb[:, 1024 + hseg * 512: 1024 + (hseg + 1) * 512],
                        in_=trow[:, sl]), wait=[c3])
                return last
            for g in range(4):
                o = g * TT3
                for pg in range(8):
                    mw = ([dm["x1use", g - 1, pg], dm["x1sq", g - 1], dm["x1dma", g - 1, pg]]
                          if g >= 1 else [dm["ccmo"]])
                    mo_ld = P.op("sync", lambda h, pg=pg, o=o: h.dma_start(
                        out=x1t[:, pg * TT3:(pg + 1) * TT3],
                        in_=A["b_mo"][pg * 128:(pg + 1) * 128, o:o + TT3]),
                        wait=mw, dma=True)
                    x_ld = P.op("sync", lambda h, pg=pg, o=o: h.dma_start(
                        out=f1t[:], in_=A["x_q"][pg * 128:(pg + 1) * 128, o:o + TT3]),
                        wait=[dm["f1use", g, pg - 1]] if pg >= 1 else ([dm["f1use", g - 1, 7]] if g >= 1 else []), dma=True)
                    dm["f1use", g, pg] = ad = P.op("vector", lambda h, pg=pg: h.tensor_tensor(
                        out=x1t[:, pg * TT3:(pg + 1) * TT3], in0=x1t[:, pg * TT3:(pg + 1) * TT3],
                        in1=f1t[:], op=ALU.add), wait=[mo_ld, x_ld])
                    dm["x1dma", g, pg] = P.op("gpsimd", lambda h, pg=pg, o=o: h.dma_start(
                        out=A["b_x1"][pg * 128:(pg + 1) * 128, o:o + TT3],
                        in_=x1t[:, pg * TT3:(pg + 1) * TT3]), wait=[ad], dma=True)
                    P.op("scalar", lambda h, pg=pg: h.activation(
                        out=x1b[:, pg * TT3:(pg + 1) * TT3],
                        in_=x1t[:, pg * TT3:(pg + 1) * TT3], func=AF.Copy), wait=[ad])
                    P.op("scalar", lambda h, pg=pg: h.activation(
                        out=x1sq[:, pg * TT3:(pg + 1) * TT3],
                        in_=x1t[:, pg * TT3:(pg + 1) * TT3], func=AF.Square))
                dm["x1sq", g] = P.mark("scalar")

                def stmm(h):
                    for pg in range(8):
                        h.matmul(out=p_n[:, 0:TT3] if False else p_rb[0:1, 0:TT3],
                                 lhsT=onescb[:], rhs=x1b[:, pg * TT3:(pg + 1) * TT3],
                                 start=(pg == 0), stop=(pg == 7))
                    for pg in range(8):
                        ins = h.matmul(out=p_rb[1:2, 0:TT3] if False else p_qkv[0:1, 0:TT3],
                                       lhsT=onescb[:], rhs=x1sq[:, pg * TT3:(pg + 1) * TT3],
                                       start=(pg == 0), stop=(pg == 7))
                    return ins
                sw = [dm["x1sq", g]] + ([dm["x1rows", g - 1]] if g >= 1 else [])
                dm["x1mm", g] = P.op("tensor", stmm, wait=sw, n=16)

                dm["x1rows", g] = q7 = t_rows(p_rb[0:1, 0:TT3], p_qkv[0:1, 0:TT3], DM,
                                              [dm["x1mm", g]])

                p_mu2 = p_sim
                p_rs2 = p_mo2

                def x1bc(h):
                    h.matmul(out=p_mu2[:], lhsT=onesb[:], rhs=trb[:, 0:512], start=True, stop=False)
                    h.matmul(out=p_mu2[:], lhsT=onesb[:], rhs=trb[:, 1024:1536],
                             start=False, stop=True)
                    h.matmul(out=p_rs2[:], lhsT=onesb[:], rhs=trb[:, 512:1024],
                             start=True, stop=False)
                    return h.matmul(out=p_rs2[:], lhsT=onesb[:], rhs=trb[:, 1536:2048],
                                    start=False, stop=True)
                bw = [q7] + ([dm["x1use", g - 1, 7]] if g >= 1 else [])
                evbc = P.op("tensor", x1bc, wait=bw, n=4)

                for pg in range(8):
                    w1 = [evbc]
                    P.op("vector", lambda h, pg=pg: h.tensor_tensor(
                        out=f2t[:], in0=x1t[:, pg * TT3:(pg + 1) * TT3],
                        in1=p_mu2[:], op=ALU.subtract), wait=w1 + [dm["f1use", g, pg]])
                    dm["x1use", g, pg] = P.op("vector", lambda h, pg=pg, g=g: h.tensor_tensor(
                        out=U[:, pg * TW + HALO + g * TT3: pg * TW + HALO + (g + 1) * TT3],
                        in0=f2t[:], in1=p_rs2[:], op=ALU.mult))
            if "x1" in dumps:
                P.op("gpsimd", lambda h: h.dma_start(out=A["dump_x1"], in_=A["b_x1"]),
                     wait=[dm["x1dma", 3, 7]], dma=True)
            if "u1" in dumps:
                for pg in range(8):
                    P.op("gpsimd", lambda h, pg=pg: h.dma_start(
                        out=A["dump_u1"][pg * 128:(pg + 1) * 128, :],
                        in_=U[:, pg * TW + HALO: pg * TW + HALO + QT]),
                        wait=[dm["x1use", 3, 7]], dma=True)

            P.fence()


            TC_QS, TC_KS, TC_G, TC_BLN, TC_B2, TC_B1A, TC_B1G = 0, 1, 2, 10, 18, 26, 48
            for pg in range(8):
                dm["Umain", pg] = dm["x1use", 3, pg]
                dm["ubdone", pg] = dm["x1use", 3, pg]
            dm["e1cc"] = dm["ccmo"]
            dm["e2cc"] = dm["ccmo"]
            dm["qkv_all"] = dm["ccmo"]
            dm["attn_done"] = dm["ccmo"]
            dm["tcols_used"] = dm["ccmo"]
            dm["wslot", 0] = dm["ccmo"]
            dm["wslot", 1] = dm["ccmo"]
            dm["wstfree"] = dm["ccmo"]
            dm["psim_free"] = dm["ccmo"]
            dm["pqkv_free"] = dm["ccmo"]
            dm["pmo2_free"] = dm["ccmo"]
            dm["pav_free"] = dm["ccmo"]
            dm["ppt_free"] = dm["ccmo"]
            dm["gbfree"] = dm["ccmo"]
            dm["prb_free"] = dm["ccmo"]
            dm["pn_free"] = dm["ccmo"]

            def wstream(src_ap, fw, slot):
                off = slot * 8 * 128
                evd_ = P.op("sync", lambda h: h.dma_start(
                    out=wst[:, 0:8 * fw].rearrange("p (k j) -> p k j", k=8), in_=src_ap),
                    wait=[dm["wstfree"], dm["wslot", slot]], dma=True)
                evc_ = P.op("scalar", lambda h: h.activation(
                    out=wstb[:, off:off + 8 * fw], in_=wst[:, 0:8 * fw],
                    func=AF.Copy), wait=[evd_])
                dm["wstfree"] = evc_
                return evc_

            for li in range(n_layers):
                r1 = P.op("sync", lambda h, li=li: h.dma_start(
                    out=rowstage[0:1, :], in_=A["qs_r"][li:li + 1, :]),
                    wait=[dm["tcols_used"]], dma=True)
                P.op("sync", lambda h, li=li: h.dma_start(
                    out=rowstage[1:2, :], in_=A["ks_r"][li:li + 1, :]), dma=True)
                P.op("sync", lambda h, li=li: h.dma_start(
                    out=rowstage[2:10, :], in_=A["g_r"][li]), dma=True)
                r4 = P.op("sync", lambda h, li=li: h.dma_start(
                    out=rowstage[10:18, :], in_=A["bln_r"][li]), dma=True)
                r5 = P.op("sync", lambda h, li=li: h.dma_start(
                    out=rowstage[18:26, :], in_=A["b2_r"][li]), dma=True)
                tr1 = P.op("tensor", lambda h: h.transpose(
                    out=p_sim[:, 0:26], in_=rowstage[0:26, :], identity=idf[0:26, 0:26]),
                    wait=[r4, r5, dm["psim_free"]])
                ev1 = P.op("vector", lambda h: h.tensor_copy(
                    out=tcols2[:, 0:26], in_=p_sim[:, 0:26]), wait=[tr1])
                r6 = P.op("sync", lambda h, li=li: h.dma_start(
                    out=rowstage[0:22, :], in_=A["b1a_r"][li]), wait=[tr1], dma=True)
                r7 = P.op("sync", lambda h, li=li: h.dma_start(
                    out=rowstage[22:44, :], in_=A["b1g_r"][li]), dma=True)
                tr2 = P.op("tensor", lambda h: h.transpose(
                    out=p_sim[:, 0:44], in_=rowstage[0:44, :], identity=idf[0:44, 0:44]),
                    wait=[r6, r7, ev1])
                evtc = P.op("vector", lambda h: h.tensor_copy(
                    out=tcols2[:, 26:70], in_=p_sim[:, 0:44]), wait=[tr2])
                dm["psim_free"] = evtc

                e1c = []
                for pg in range(8):
                    e1c.append(P.op("gpsimd", lambda h, pg=pg: h.dma_start(
                        out=A["b_ex_in"][pg * 128:(pg + 1) * 128, :],
                        in_=U[:, pg * TW + QT: pg * TW + TW]),
                        wait=[dm["Umain", pg], dm["e1cc"]], dma=True))
                cc1 = P.op("gpsimd", lambda h: h.collective_compute(
                    "AllGather", ALU.bypass, replica_groups=GROUPS,
                    ins=[A["b_ex_in"]], outs=[A["b_ex_out"]]),
                    wait=e1c, cc=True)
                dm["e1cc"] = cc1
                e1m = []
                slotbufs = [qnf, tmpf, s3f, f1t]
                for pg in range(8):
                    lds = []
                    for s in range(4):
                        lds.append(P.op("gpsimd", lambda h, pg=pg, s=s: h.dma_start(
                            out=slotbufs[s][:, 0:HALO],
                            in_=A["b_ex_out"][s, pg * 128:(pg + 1) * 128, :]),
                            wait=[cc1] + ([dm["e1prev"]] if "e1prev" in dm else []), dma=True))
                    c0 = P.op("vector", lambda h, pg=pg: h.tensor_scalar(
                        out=U[:, pg * TW: pg * TW + HALO], in0=slotbufs[0][:, 0:HALO],
                        scalar1=pmask_sb[:, 0:1], scalar2=None, op0=ALU.mult),
                        wait=lds + [ev_pm, dm["ubdone", pg]])
                    for s in range(1, 4):
                        c0 = P.op("vector", lambda h, pg=pg, s=s: h.scalar_tensor_tensor(
                            out=U[:, pg * TW: pg * TW + HALO], in0=slotbufs[s][:, 0:HALO],
                            scalar=pmask_sb[:, s:s + 1],
                            in1=U[:, pg * TW: pg * TW + HALO], op0=ALU.mult, op1=ALU.add))
                    dm["e1mm", pg] = c0
                    dm["e1prev"] = c0
                    e1m.append(c0)

                ubev = []
                ms_ev = None
                for pg in range(8):
                    if pg < 4:
                        ubev.append(P.op("scalar", lambda h, pg=pg: h.activation(
                            out=ub[:, pg * TW: (pg + 1) * TW], in_=U[:, pg * TW:(pg + 1) * TW],
                            func=AF.Copy), wait=[e1m[pg], dm["qkv_all"]]))
                    else:
                        ms_ev = P.op("vector", lambda h, pg=pg: h.memset(
                            ub[:, pg * TW: pg * TW + 1], 0.0), wait=[dm["qkv_all"]])
                        ubev.append(P.op("scalar", lambda h, pg=pg: h.activation(
                            out=ub[:, pg * TW + 1: (pg + 1) * TW],
                            in_=U[:, pg * TW: (pg + 1) * TW - 1],
                            func=AF.Copy), wait=[e1m[pg], dm["qkv_all"]]))
                for pg in range(8):
                    dm["ubdone", pg] = ubev[-1]
                ubcast = ubev[-1]
                ubw2 = ms_ev
                if "ubd" in dumps and li == n_layers - 1:
                    ude = None
                    for pg in range(8):
                        ude = P.op("gpsimd", lambda h, pg=pg: h.dma_start(
                            out=A["dump_ubd"][pg * 128:(pg + 1) * 128, :],
                            in_=ub[:, pg * TW:(pg + 1) * TW]),
                            wait=[ubcast, ubw2], dma=True)
                    dm["qkdump"] = ude
                    dm["qkdump2"] = ude

                for hg in range(8):
                    evq = {}
                    for pi, (part, dst) in enumerate([(0, qh), (1, kh), (2, vh)]):
                        slot = (hg * 3 + pi) % 2
                        wev = wstream(A["wqkv"][li][:, part * DM + hg * 128:
                                                    part * DM + (hg + 1) * 128]
                                      .rearrange("(k p) j -> p k j", p=128), 128, slot)
                        pe_pred = dm["attn_done"] if hg > 0 or li > 0 else dm["ccmo"]
                        lastmm = None
                        for (tt0, tl) in TW_TILES:
                            mm = P.op("tensor", lambda h, slot=slot, tt0=tt0, tl=tl: [
                                h.matmul(out=p_qkv[:, 0:tl],
                                         lhsT=wstb[:, slot * 8 * 128 + k * 128: slot * 8 * 128 + (k + 1) * 128],
                                         rhs=ub[:, k * TW + tt0: k * TW + tt0 + tl],
                                         start=(k == 0), stop=(k == 7)) for k in range(8)][-1],
                                wait=[wev, ubcast, ubw2, dm["pqkv_free"], pe_pred], n=8)
                            evac = P.op("scalar", lambda h, dst=dst, tt0=tt0, tl=tl: h.activation(
                                out=dst[:, tt0:tt0 + tl], in_=p_qkv[:, 0:tl], func=AF.Copy),
                                wait=[mm] + ([dm["attn_done"]] if (tt0 == 0 and (hg > 0 or li > 0)) else []) +
                                ([dm["qkdump"], dm["qkdump2"]] if (tt0 == 0 and "qkdump" in dm) else []))
                            dm["pqkv_free"] = evac
                            lastmm = evac
                        evq[part] = lastmm
                        dm["wslot", slot] = lastmm
                    dm["qkv_all"] = evq[2]

                    for part, srcb, tabc, tabs_, scol in ((0, qh, "cs", "ss", TC_QS),
                                                         (1, kh, "ci", "si", TC_KS)):
                        for (tt0, tl) in TW_TILES:
                            sq = P.op("scalar", lambda h, srcb=srcb, tt0=tt0, tl=tl: h.activation(
                                out=sqb[:, 0:tl], in_=srcb[:, tt0:tt0 + tl], func=AF.Square),
                                wait=[evq[part], dm["pn_free"]])
                            nm1 = P.op("tensor", lambda h, tl=tl: h.matmul(
                                out=p_n[:, 0:tl], lhsT=ones2c[:], rhs=sqb[:, 0:tl],
                                start=True, stop=True), wait=[sq, dm["prb_free"]])
                            w1 = P.op("scalar", lambda h, tl=tl: h.activation(
                                out=nrow[:, 0:tl], in_=p_n[:, 0:tl], func=AF.Sqrt), wait=[nm1])
                            dm["pn_free"] = w1
                            w2 = P.op("vector", lambda h, tl=tl: h.tensor_scalar(
                                out=nrow[:, 0:tl], in0=nrow[:, 0:tl], scalar1=1e-12,
                                scalar2=None, op0=ALU.max), wait=[w1])
                            w3 = P.op("vector", lambda h, tl=tl: h.reciprocal(
                                out=nrow[:, 0:tl], in_=nrow[:, 0:tl]), wait=[w2])
                            w4 = P.op("vector", lambda h, tl=tl: h.tensor_copy(
                                out=nrb[:, 0:tl], in_=nrow[:, 0:tl]), wait=[w3])
                            w5 = P.op("scalar", lambda h, tl=tl: h.activation(
                                out=nrow[:, 512:512 + tl], in_=nrb[:, 0:tl], func=AF.Copy),
                                wait=[w4])
                            w6 = P.op("vector", lambda h, tl=tl: h.tensor_tensor(
                                out=nrow[:, 512:512 + tl], in0=nrow[:, 0:tl],
                                in1=nrow[:, 512:512 + tl], op=ALU.subtract), wait=[w5])
                            w7 = P.op("vector", lambda h, tl=tl: h.tensor_copy(
                                out=nrb[:, 512:512 + tl], in_=nrow[:, 512:512 + tl]), wait=[w6])
                            nb1 = P.op("tensor", lambda h, tl=tl: (
                                h.matmul(out=p_rb[:, 0:tl], lhsT=ones2r[:], rhs=nrb[:, 0:tl],
                                         start=True, stop=False),
                                h.matmul(out=p_rb[:, 0:tl], lhsT=ones2r[:],
                                         rhs=nrb[:, 512:512 + tl], start=False, stop=True))[-1],
                                wait=[w7], n=2)
                            st1 = P.op("vector", lambda h, srcb=srcb, scol=scol, tt0=tt0, tl=tl:
                                       h.scalar_tensor_tensor(
                                           out=qnf[:, 0:tl], in0=srcb[:, tt0:tt0 + tl],
                                           scalar=tcols2[:, scol:scol + 1], in1=p_rb[:, 0:tl],
                                           op0=ALU.mult, op1=ALU.mult), wait=[nb1, evtc])
                            dm["prb_free"] = st1
                            rt = []
                            for (po, so, sc) in ((0, 32, -1.0), (32, 0, 1.0),
                                                 (64, 96, -1.0), (96, 64, 1.0)):
                                rt.append(P.op("scalar", lambda h, po=po, so=so, sc=sc, tl=tl:
                                               h.activation(out=tmpf[po:po + 32, 0:tl],
                                                            in_=qnf[so:so + 32, 0:tl],
                                                            func=AF.Copy, scale=sc), wait=[st1]))
                            m1 = P.op("vector", lambda h, tabc=tabc, tt0=tt0, tl=tl: h.tensor_tensor(
                                out=s3f[:, 0:tl], in0=qnf[:, 0:tl],
                                in1=tabs[tabc][:, tt0:tt0 + tl], op=ALU.mult),
                                wait=[st1, dm["tab", tabc]])
                            m2 = P.op("vector", lambda h, tabs_=tabs_, tt0=tt0, tl=tl: h.tensor_tensor(
                                out=qnf[:, 0:tl], in0=tmpf[:, 0:tl],
                                in1=tabs[tabs_][:, tt0:tt0 + tl], op=ALU.mult),
                                wait=rt + [dm["tab", tabs_]])
                            m3 = P.op("vector", lambda h, srcb=srcb, tt0=tt0, tl=tl: h.tensor_tensor(
                                out=srcb[:, tt0:tt0 + tl], in0=s3f[:, 0:tl], in1=qnf[:, 0:tl],
                                op=ALU.add), wait=[m1, m2])
                            evq[part] = m3
                    qkprep = evq[1]
                    if "qk" in dumps and li == n_layers - 1 and hg == 0:
                        dq = P.op("gpsimd", lambda h: h.dma_start(
                            out=A["dump_qk"][0:1].squeeze(0), in_=qh[:]), wait=[qkprep], dma=True)
                        dq2 = P.op("gpsimd", lambda h: h.dma_start(
                            out=A["dump_qk"][1:2].squeeze(0), in_=kh[:]), wait=[qkprep], dma=True)
                        dq3 = P.op("gpsimd", lambda h: h.dma_start(
                            out=A["dump_qk"][2:3].squeeze(0), in_=vh[:]), wait=[evq[2]], dma=True)
                        dm["qkdump"] = dq3
                        dm["qkdump2"] = dq2

                    prev_vt = dm["ppt_free"]
                    for hd in range(2):
                        for c in range(18):
                            trv = P.op("tensor", lambda h, hd=hd, c=c: h.transpose(
                                out=p_pt[:, 0:64],
                                in_=vh[hd * 64:(hd + 1) * 64, 2 + c * 128: 2 + (c + 1) * 128],
                                identity=idb[hd * 64:hd * 64 + 64, hd * 64:hd * 64 + 64]),
                                wait=[evq[2], prev_vt, dm["gbfree"]])
                            prev_vt = P.op("scalar", lambda h, hd=hd, c=c: h.activation(
                                out=VT[:, (hd * 18 + c) * DH:(hd * 18 + c + 1) * DH],
                                in_=p_pt[:, 0:64], func=AF.Copy), wait=[trv])
                    vt_all = prev_vt
                    dm["ppt_free"] = vt_all

                    for w in range(8):
                        for hd in range(2):
                            hs0 = hd * 64
                            for qhalf in range(2):
                                qcol = HALO + 256 * w + 128 * qhalf
                                kcol = 2 + 256 * w
                                mi = (2 + qhalf) if w == 0 else qhalf
                                smm = P.op("tensor", lambda h, hs0=hs0, qcol=qcol, kcol=kcol:
                                           h.matmul(out=p_sim[:],
                                                    lhsT=qh[hs0:hs0 + 64, qcol:qcol + 128],
                                                    rhs=kh[hs0:hs0 + 64, kcol:kcol + 512],
                                                    start=True, stop=True),
                                           wait=[qkprep, dm["psim_free"]])
                                mk = P.op("vector", lambda h, mi=mi: h.scalar_tensor_tensor(
                                    out=p_sim[:], in0=p_sim[:], scalar=8.0,
                                    in1=maskb[:, mi * 512:(mi + 1) * 512],
                                    op0=ALU.mult, op1=ALU.add), wait=[smm, dm["masks"]])
                                nx = P.op("vector", lambda h: h.reduce_max(
                                    out=negmax[:], in_=p_sim[:], axis=AX.X, negate=True),
                                    wait=[mk])
                                ex = P.op("scalar", lambda h: h.activation(
                                    out=Pb[:], in_=p_sim[:], func=AF.Exp, bias=negmax[:, 0:1],
                                    accum_out=lrow[:]), wait=[nx])
                                dm["psim_free"] = ex
                                rc = P.op("vector", lambda h: h.reciprocal(
                                    out=rlcol[:], in_=lrow[:]), wait=[ex])
                                psc = P.op("vector", lambda h: h.tensor_scalar(
                                    out=Pb2[:], in0=Pb[:], scalar1=rlcol[:, 0:1], scalar2=None,
                                    op0=ALU.mult), wait=[rc])
                                pte = dm["ppt_free"]
                                for c in range(4):
                                    ptr = P.op("tensor", lambda h, c=c: h.transpose(
                                        out=p_pt[:, 0:128],
                                        in_=Pb2[:, c * 128:(c + 1) * 128],
                                        identity=idb[:]),
                                        wait=[psc, pte, vt_all])
                                    pte = P.op("scalar", lambda h, c=c: h.activation(
                                        out=PTs[:, c * 128:(c + 1) * 128], in_=p_pt[:, 0:128],
                                        func=AF.Copy), wait=[ptr])
                                dm["ppt_free"] = pte
                                avm = P.op("tensor", lambda h, hs0=hs0, w=w, hd=hd, qhalf=qhalf: [
                                    h.matmul(out=p_av[0:64, qhalf * 128:(qhalf + 1) * 128],
                                             lhsT=VT[:, (hd * 18 + 2 * w + c) * DH:
                                                     (hd * 18 + 2 * w + c + 1) * DH],
                                             rhs=PTs[:, c * 128:(c + 1) * 128],
                                             start=(c == 0), stop=(c == 3)) for c in range(4)][-1],
                                    wait=[pte, dm["pav_free"]], n=4)
                                ae_q = P.op("scalar", lambda h, hg=hg, w=w, hd=hd, qhalf=qhalf:
                                            h.activation(
                                    out=avb[hd * 64:(hd + 1) * 64,
                                            hg * QT + w * 256 + qhalf * 128:
                                            hg * QT + w * 256 + qhalf * 128 + 128],
                                    in_=p_av[0:64, qhalf * 128:(qhalf + 1) * 128],
                                    func=AF.Copy), wait=[avm])
                                dm["pav_free"] = ae_q
                        dm["avev", hg, w] = dm["pav_free"]
                    dm["attn_done"] = dm["avev", hg, 7]
                av_all = [dm["avev", hgx, 7] for hgx in range(8)]

                if "av" in dumps and li == n_layers - 1:
                    for hg in range(8):
                        P.op("gpsimd", lambda h, hg=hg: h.dma_start(
                            out=A["dump_av"][hg * 128:(hg + 1) * 128, :],
                            in_=avb[:, hg * QT:(hg + 1) * QT]),
                            wait=[dm["avev", hg, 7]], dma=True)

                for do in range(8):
                    slot = do % 2
                    wev = wstream(A["wo"][li][:, do * 128:(do + 1) * 128]
                                  .rearrange("(k p) j -> p k j", p=128), 128, slot)
                    for ti, (mt0, tl) in enumerate([(i * 512, 512) for i in range(4)]):
                        mm = P.op("tensor", lambda h, slot=slot, mt0=mt0: [
                            h.matmul(out=p_mo2[:],
                                     lhsT=wstb[:, slot * 8 * 128 + k * 128: slot * 8 * 128 + (k + 1) * 128],
                                     rhs=avb[:, k * QT + mt0: k * QT + mt0 + 512],
                                     start=(k == 0), stop=(k == 7)) for k in range(8)][-1],
                            wait=av_all + [wev, dm["pmo2_free"]])
                        stt = P.op("vector", lambda h, do=do, mt0=mt0: h.scalar_tensor_tensor(
                            out=U[:, do * TW + HALO + mt0: do * TW + HALO + mt0 + 512],
                            in0=p_mo2[:], scalar=0.0,
                            in1=U[:, do * TW + HALO + mt0: do * TW + HALO + mt0 + 512],
                            op0=ALU.add, op1=ALU.add),
                            wait=[mm, dm["Umain", do]] + e1c)
                        dm["pmo2_free"] = stt
                        dm["Umain", do] = stt
                    dm["wslot", slot] = dm["Umain", do]

                if "uam" in dumps and li == n_layers - 2:
                    for pg in range(8):
                        dm["Umain", pg] = P.op("gpsimd", lambda h, pg=pg: h.dma_start(
                            out=A["dump_uam"][pg * 128:(pg + 1) * 128, :],
                            in_=U[:, pg * TW + HALO: pg * TW + HALO + QT]),
                            wait=[dm["Umain", pg]], dma=True)
                if "ua" in dumps and li == n_layers - 1:
                    for pg in range(8):
                        P.op("gpsimd", lambda h, pg=pg: h.dma_start(
                            out=A["dump_ua"][pg * 128:(pg + 1) * 128, :],
                            in_=U[:, pg * TW + HALO: pg * TW + HALO + QT]),
                            wait=[dm["Umain", pg]], dma=True)

                e2c = []
                for pg in range(8):
                    e2c.append(P.op("gpsimd", lambda h, pg=pg: h.dma_start(
                        out=A["b_e2_in"][pg * 128:(pg + 1) * 128, :],
                        in_=U[:, pg * TW + TW - 1: pg * TW + TW]),
                        wait=[dm["Umain", pg], dm["e2cc"]], dma=True))
                cc2 = P.op("gpsimd", lambda h: h.collective_compute(
                    "AllGather", ALU.bypass, replica_groups=GROUPS,
                    ins=[A["b_e2_in"]], outs=[A["b_e2_out"]]), wait=e2c, cc=True)
                dm["e2cc"] = cc2
                e2m = []
                for pg in range(8):
                    lds2 = []
                    for s in range(4):
                        lds2.append(P.op("gpsimd", lambda h, pg=pg, s=s: h.dma_start(
                            out=slotbufs[s][:, 256:257],
                            in_=A["b_e2_out"][s, pg * 128:(pg + 1) * 128, :]),
                            wait=[cc2] + ([dm["e2prev"]] if "e2prev" in dm else []),
                            dma=True))
                    c0 = P.op("vector", lambda h, pg=pg: h.tensor_scalar(
                        out=U[:, pg * TW + HALO - 1: pg * TW + HALO], in0=slotbufs[0][:, 256:257],
                        scalar1=pmask_sb[:, 0:1], scalar2=None, op0=ALU.mult),
                        wait=lds2 + [ev_pm, dm["ubdone", pg]])
                    for s in range(1, 4):
                        c0 = P.op("vector", lambda h, pg=pg, s=s: h.scalar_tensor_tensor(
                            out=U[:, pg * TW + HALO - 1: pg * TW + HALO],
                            in0=slotbufs[s][:, 256:257], scalar=pmask_sb[:, s:s + 1],
                            in1=U[:, pg * TW + HALO - 1: pg * TW + HALO],
                            op0=ALU.mult, op1=ALU.add))
                    dm["e2mm", pg] = c0
                    dm["e2prev"] = c0
                    e2m.append(c0)

                xnbb = ub
                xnall = None
                for ti in range(4):
                    mt0 = ti * 512
                    stev = []
                    for pg in range(8):
                        sh = HALO + mt0 if pg < 4 else HALO + mt0 - 1
                        cw = [e2m[pg], dm["qkv_all"]] if ti == 0 else []
                        if "ubtprev" in dm:
                            cw = cw + [dm["ubtprev"]]
                        c1 = P.op("scalar", lambda h, pg=pg, sh=sh: h.activation(
                            out=ubt[:], in_=U[:, pg * TW + sh: pg * TW + sh + 512],
                            func=AF.Copy), wait=cw)
                        c2 = P.op("scalar", lambda h, pg=pg, sh=sh: h.activation(
                            out=sqb[:], in_=U[:, pg * TW + sh: pg * TW + sh + 512],
                            func=AF.Square))
                        m1 = P.op("tensor", lambda h, pg=pg: h.matmul(
                            out=p_n[0:1, :], lhsT=onescb[:], rhs=ubt[:],
                            start=(pg == 0), stop=(pg == 7)), wait=[c1, c2] +
                            ([dm["ffrow", ti - 1]] if ti >= 1 and pg == 0 else []))
                        m2 = P.op("tensor", lambda h, pg=pg: h.matmul(
                            out=p_sim[0:1, :], lhsT=onescb[:], rhs=sqb[:],
                            start=(pg == 0), stop=(pg == 7)),
                            wait=[dm["psim_free"]] if pg == 0 else [])
                        dm["ubtprev"] = m2
                        stev.append(m2)
                    f7_ = t_rows(p_n[0:1, :], p_sim[0:1, :], DM, stev)
                    dm["ffrow", ti] = f7_
                    fbc = P.op("tensor", lambda h: (
                        h.matmul(out=p_rb[:], lhsT=onesb[:], rhs=trb[:, 0:512],
                                 start=True, stop=False),
                        h.matmul(out=p_rb[:], lhsT=onesb[:], rhs=trb[:, 1024:1536],
                                 start=False, stop=True),
                        h.matmul(out=p_sim[:], lhsT=onesb[:], rhs=trb[:, 512:1024],
                                 start=True, stop=False),
                        h.matmul(out=p_sim[:], lhsT=onesb[:], rhs=trb[:, 1536:2048],
                                 start=False, stop=True))[-1],
                        wait=[f7_, dm["prb_free"], dm["psim_free"]], n=4)
                    xnev = []
                    for pg in range(8):
                        sh = HALO + mt0 if pg < 4 else HALO + mt0 - 1
                        x1_ = P.op("vector", lambda h, pg=pg, sh=sh: h.tensor_tensor(
                            out=f1t[:], in0=U[:, pg * TW + sh: pg * TW + sh + 512],
                            in1=p_rb[:], op=ALU.subtract), wait=[fbc])
                        x2_ = P.op("vector", lambda h: h.tensor_tensor(
                            out=f2t[:], in0=f1t[:], in1=p_sim[:], op=ALU.mult))
                        x3_ = P.op("vector", lambda h, pg=pg, ti=ti: h.scalar_tensor_tensor(
                            out=xnbb[:, pg * QT + ti * 512: pg * QT + ti * 512 + 512],
                            in0=f2t[:],
                            scalar=tcols2[:, TC_G + pg:TC_G + pg + 1],
                            in1=tcols2[:, TC_BLN + pg:TC_BLN + pg + 1].broadcast_to([128, 512]),
                            op0=ALU.mult, op1=ALU.add))
                        xnev.append(x3_)
                    dm["prb_free"] = xnev[-1]
                    dm["psim_free"] = xnev[-1]
                    xnall = xnev[-1]

                if "xnd" in dumps and li == n_layers - 2:
                    xde = None
                    for pg in range(8):
                        xde = P.op("gpsimd", lambda h, pg=pg: h.dma_start(
                            out=A["dump_xnd"][pg * 128:(pg + 1) * 128, :],
                            in_=xnbb[:, pg * QT:(pg + 1) * QT]), wait=[xnall], dma=True)
                    dm["xnb_last_rd"] = xde

                for ti in range(4):
                    mt0 = ti * 512
                    for j, (fo, fw) in enumerate(A_TILES):
                        slot = j % 2
                        gev = wstream(A["w1"][li][:, DIN + fo: DIN + fo + fw]
                                      .rearrange("(k p) j -> p k j", p=128), fw, slot)
                        gmm = P.op("tensor", lambda h, slot=slot, fw=fw, ti=ti: [
                            h.matmul(out=p_qkv[0:fw, :],
                                     lhsT=wstb[0:128, slot * 8 * 128 + k * fw: slot * 8 * 128 + (k + 1) * fw],
                                     rhs=xnbb[:, k * QT + ti * 512: k * QT + ti * 512 + 512],
                                     start=(k == 0), stop=(k == 7)) for k in range(8)][-1],
                            wait=[xnall, gev, dm["pqkv_free"]], n=8)
                        gel = P.op("scalar", lambda h, j=j, fw=fw: h.activation(
                            out=gb_t[0:fw, :], in_=p_qkv[0:fw, :], func=AF.Gelu,
                            bias=tcols2[0:fw, TC_B1G + j:TC_B1G + j + 1]), wait=[gmm, evtc])
                        dm["pqkv_free"] = gel
                        aev = wstream(A["w1"][li][:, fo: fo + fw]
                                      .rearrange("(k p) j -> p k j", p=128), fw, 1 - slot)
                        amm = P.op("tensor", lambda h, slot=slot, fw=fw, ti=ti: [
                            h.matmul(out=p_qkv[0:fw, :],
                                     lhsT=wstb[0:128, (1 - slot) * 8 * 128 + k * fw: (1 - slot) * 8 * 128 + (k + 1) * fw],
                                     rhs=xnbb[:, k * QT + ti * 512: k * QT + ti * 512 + 512],
                                     start=(k == 0), stop=(k == 7)) for k in range(8)][-1],
                            wait=[aev, dm["pqkv_free"]], n=8)
                        pw_ = [amm, gel] + ([dm["prodread", j]] if ("prodread", j) in dm else [])
                        if ti == 0 and j == 0:
                            pw_ += [dm["Umain", d_] for d_ in range(8)]
                        prd = P.op("vector", lambda h, j=j, fw=fw: h.scalar_tensor_tensor(
                            out=prodb[0:fw, j * 512:(j + 1) * 512], in0=p_qkv[0:fw, :],
                            scalar=tcols2[0:fw, TC_B1A + j:TC_B1A + j + 1], in1=gb_t[0:fw, :],
                            op0=ALU.add, op1=ALU.mult), wait=pw_)
                        dm["pqkv_free"] = prd
                        dm["gbfree"] = prd
                        dm["wslot", slot] = gmm
                        dm["wslot", 1 - slot] = amm
                        dm["prodmade", j] = prd
                        dm["xnb_last_rd"] = amm

                    if "prodd" in dumps and li == n_layers - 2 and ti == 3:
                        pde = P.op("gpsimd", lambda h: h.dma_start(
                            out=A["dump_prodd"], in_=prodb[:, 0:22 * 512]),
                            wait=[dm["prodmade", j] for j in range(22)], dma=True)
                        for j in range(22):
                            dm["prodread", j] = pde
                    for do in range(8):
                        kgroups = [(0, 8), (8, 8), (16, 6)]
                        mm2 = None
                        for gi, (k0, kn) in enumerate(kgroups):
                            slot = gi % 2
                            rows0 = k0 * 128
                            rows1 = min(DIN, (k0 + kn) * 128)
                            nfull = (rows1 - rows0) // 128
                            rg = (rows1 - rows0) % 128
                            devs = []
                            if nfull > 0:
                                devs.append(P.op("sync", lambda h, rows0=rows0, nfull=nfull, do=do, li=li:
                                                 h.dma_start(
                                    out=wst[:, 0:nfull * 128].rearrange(
                                        "p (k j) -> p k j", k=nfull),
                                    in_=A["w2"][li][rows0:rows0 + nfull * 128,
                                                    do * 128:(do + 1) * 128]
                                    .rearrange("(k p) j -> p k j", p=128)),
                                    wait=[dm["wstfree"], dm["wslot", slot]], dma=True))
                            if rg:
                                devs.append(P.op("sync", lambda h, rows0=rows0, nfull=nfull, rg=rg, do=do, li=li:
                                                 h.dma_start(
                                    out=wst[0:rg, nfull * 128:(nfull + 1) * 128],
                                    in_=A["w2"][li][rows0 + nfull * 128: rows0 + nfull * 128 + rg,
                                                    do * 128:(do + 1) * 128]),
                                    wait=[dm["wstfree"], dm["wslot", slot]], dma=True))
                            cev = P.op("scalar", lambda h, slot=slot, kn=kn: h.activation(
                                out=wstb[:, slot * 8 * 128: slot * 8 * 128 + kn * 128],
                                in_=wst[:, 0:kn * 128],
                                func=AF.Copy), wait=devs)
                            dm["wstfree"] = cev

                            def mm2g(h, k0=k0, kn=kn, slot=slot):
                                for kk in range(kn):
                                    k = k0 + kk
                                    fo, fw = A_TILES[k]
                                    ins = h.matmul(
                                        out=p_mo2[:],
                                        lhsT=wstb[0:fw, slot * 8 * 128 + kk * 128: slot * 8 * 128 + kk * 128 + 128],
                                        rhs=prodb[0:fw, k * 512:(k + 1) * 512],
                                        start=(k == 0), stop=(k == 21))
                                return ins
                            mmw = [cev] + [dm["prodmade", j] for j in range(k0, k0 + kn)]
                            if gi == 0:
                                mmw.append(dm["pmo2_free"])
                            mm2 = P.op("tensor", mm2g, wait=mmw, n=kn)
                            dm["wslot", slot] = mm2
                        stt2 = P.op("vector", lambda h, do=do, mt0=mt0: h.scalar_tensor_tensor(
                            out=U[:, do * TW + HALO + mt0: do * TW + HALO + mt0 + 512],
                            in0=p_mo2[:], scalar=tcols2[:, TC_B2 + do:TC_B2 + do + 1],
                            in1=U[:, do * TW + HALO + mt0: do * TW + HALO + mt0 + 512],
                            op0=ALU.add, op1=ALU.add), wait=[mm2, dm["Umain", do], xnall])
                        dm["pmo2_free"] = stt2
                        dm["Umain", do] = stt2
                        dm["wslot", 0] = mm2
                        dm["wslot", 1] = mm2
                    for j in range(22):
                        dm["prodread", j] = dm["pmo2_free"]
                    dm["prodfree", 0] = dm["pmo2_free"]
                dm["tcols_used"] = dm["pmo2_free"]
                dm["qkv_all"] = dm["xnb_last_rd"]
                if "umid" in dumps and li == n_layers - 2:
                    for pg in range(8):
                        dm["Umain", pg] = P.op("gpsimd", lambda h, pg=pg: h.dma_start(
                            out=A["dump_umid"][pg * 128:(pg + 1) * 128, :],
                            in_=U[:, pg * TW + HALO: pg * TW + HALO + QT]),
                            wait=[dm["Umain", pg]], dma=True)
                if "u" in dumps and li == n_layers - 1:
                    for pg in range(8):
                        P.op("gpsimd", lambda h, pg=pg: h.dma_start(
                            out=A["dump_u"][pg * 128:(pg + 1) * 128, :],
                            in_=U[:, pg * TW + HALO: pg * TW + HALO + QT]),
                            wait=[dm["Umain", pg]], dma=True)

            P.fence()

            for g in range(4):
                o = g * TT3
                for pg in range(8):
                    xw = [dm["flast"]] if "flast" in dm else []
                    x_ld = P.op("sync", lambda h, pg=pg, o=o: h.dma_start(
                        out=f1t[:], in_=A["b_x1"][pg * 128:(pg + 1) * 128, o:o + TT3]),
                        wait=xw, dma=True)
                    dm["fx1", g, pg] = ad = P.op("vector", lambda h, pg=pg, o=o, g=g: h.tensor_tensor(
                        out=f1t[:], in0=f1t[:],
                        in1=U[:, pg * TW + HALO + o: pg * TW + HALO + o + TT3], op=ALU.add),
                        wait=[x_ld])
                    dm["odma", g, pg] = P.op("gpsimd", lambda h, pg=pg, o=o: h.dma_start(
                        out=A["o_out"][pg * 128:(pg + 1) * 128, o:o + TT3],
                        in_=f1t[:]), wait=[ad], dma=True)
                    dm["flast"] = dm["odma", g, pg]
            def _final_wait(h):
                for e in ENGS:
                    for s_ in P.dsems[e]:
                        h.wait_ge(s_.h, s_.n)
            P.raw("gpsimd", _final_wait)

            P.flush()
    return nc, A



def _rotary_tables_np(pos):
    half = np.arange(0, DH, 2, dtype=np.float32)
    inv_freq = (1.0 / 10000.0 ** (half / DH)).astype(np.float32)
    p = pos.astype(np.float32)
    freqs = p[:, None] * inv_freq[None, :]
    freqs = np.concatenate([freqs, freqs], -1)
    base = ((half + 0.4 * DH) / (1.4 * DH)).astype(np.float32)
    power = (p - T // 2) / (WIN // 2)
    scale = base[None, :] ** power[:, None]
    scale = np.concatenate([scale, scale], -1)
    return freqs.astype(np.float32), scale.astype(np.float32)


def _host_prepare(inputs):
    f32 = np.float32
    bf = ml_dtypes.bfloat16
    x = np.asarray(inputs["L0_attn_x"], f32)
    W_in = np.asarray(inputs["L0_attn_W_in"], f32)
    conv_w = np.asarray(inputs["L0_attn_conv_w"], f32)
    conv_b = np.asarray(inputs["L0_attn_conv_b"], f32)
    W_x = np.asarray(inputs["L0_attn_W_x"], f32)
    W_dt = np.asarray(inputs["L0_attn_W_dt"], f32)
    b_dt = np.asarray(inputs["L0_attn_b_dt"], f32)
    A_log = np.asarray(inputs["L0_attn_A_log"], f32)
    D_par = np.asarray(inputs["L0_attn_D_param"], f32)
    W_out = np.asarray(inputs["L0_attn_W_out"], f32)
    ln_w = np.asarray(inputs["L0_attn_ln_w"], f32)
    ln_b = np.asarray(inputs["L0_attn_ln_b"], f32)
    Wqkv = np.ascontiguousarray(np.asarray(inputs["L0_ffn_Wqkv"], f32))
    q_scale = np.asarray(inputs["L0_ffn_q_scale"], f32)
    k_scale = np.asarray(inputs["L0_ffn_k_scale"], f32)
    Wo = np.ascontiguousarray(np.asarray(inputs["L0_ffn_Wo"], f32))
    ln_g = np.asarray(inputs["L0_ffn_ln_g"], f32)
    ln_b2 = np.asarray(inputs["L0_ffn_ln_b"], f32)
    W1 = np.ascontiguousarray(np.asarray(inputs["L0_ffn_W1"], f32))
    b1 = np.asarray(inputs["L0_ffn_b1"], f32)
    W2 = np.ascontiguousarray(np.asarray(inputs["L0_ffn_W2"], f32))
    b2 = np.asarray(inputs["L0_ffn_b2"], f32)

    o2c = np.zeros((128, 2), ml_dtypes.bfloat16)
    o2c[0:64, 0] = 1; o2c[64:128, 1] = 1
    o2r = np.zeros((2, 128), ml_dtypes.bfloat16)
    o2r[0, 0:64] = 1; o2r[1, 64:128] = 1
    ident = np.eye(128, dtype=f32)
    identb = np.eye(128, dtype=f32).astype(bf)
    qs_r = np.tile(q_scale, (1, 2)).astype(f32)
    ks_r = np.tile(k_scale, (1, 2)).astype(f32)
    g_r = ln_g.reshape(NL, 8, 128)
    bln_r = ln_b2.reshape(NL, 8, 128)
    b2_r = b2.reshape(NL, 8, 128)
    b1a = np.zeros((NL, 22 * 128), f32); b1a[:, :DIN] = b1[:, :DIN]
    b1g = np.zeros((NL, 22 * 128), f32); b1g[:, :DIN] = b1[:, DIN:]
    b1a_r = b1a.reshape(NL, 22, 128)
    b1g_r = b1g.reshape(NL, 22, 128)

    masks_reg = np.empty((2, 128, 512), f32)
    j = np.arange(512)
    for qh_ in range(2):
        for r in range(128):
            i = qh_ * 128 + r
            ok = (j >= i) & (j <= i + 256)
            masks_reg[qh_, r] = np.where(ok, 0.0, -1.0e38).astype(f32)
    masks_first = masks_reg.copy()
    masks_first[:, :, :256] = -1.0e38

    in_maps = []
    for r in range(8):
        b, q = r // 4, r % 4
        sl = slice(D4 * q, D4 * (q + 1))
        xT = np.ascontiguousarray(x[b].T)
        win_rows = np.concatenate([W_in[sl], W_in[DI + D4 * q: DI + D4 * (q + 1)]], 0)
        winT = np.ascontiguousarray(win_rows.T)
        wxT = np.ascontiguousarray(W_x[:, sl].T)
        wdt_r = W_dt[sl, 0].reshape(PGM, 128)
        bdt_r = b_dt[sl].reshape(PGM, 128)
        cw = conv_w[sl, 0, :]
        convw_r = np.stack([cw[pg * 128:(pg + 1) * 128, tap]
                            for tap in range(4) for pg in range(PGM)], 0)
        convb_r = conv_b[sl].reshape(PGM, 128)
        d_r = D_par[sl].reshape(PGM, 128)
        lnw_r = ln_w[sl].reshape(PGM, 128)
        lnb_r = ln_b[sl].reshape(PGM, 128)
        woutT = np.ascontiguousarray(W_out[:, sl].T)

        start = q * QT
        pos = np.arange(start - HALO, start + QT)
        posc = np.clip(pos, 0, T - 1)
        fr, sc = _rotary_tables_np(posc)
        cs = np.tile((np.cos(fr) * sc).T, (2, 1)).astype(bf)
        ss = np.tile((np.sin(fr) * sc).T, (2, 1)).astype(bf)
        ci = np.tile((np.cos(fr) / sc).T, (2, 1)).astype(bf)
        si = np.tile((np.sin(fr) / sc).T, (2, 1)).astype(bf)

        mk = np.concatenate([masks_reg, masks_first if q == 0 else masks_reg], 0).astype(bf)
        pm = np.zeros((128, 4), f32)
        if q > 0:
            pm[:, q - 1] = 1.0
        in_maps.append(dict(
            xT=xT, winT=winT, wxT=wxT, wdt_r=wdt_r, bdt_r=bdt_r, convw_r=convw_r,
            convb_r=convb_r, d_r=d_r, lnw_r=lnw_r, lnb_r=lnb_r,
            alog=A_log.reshape(1, DS), woutT=woutT, ident=ident, identb=identb,
            wqkv=Wqkv, wo=Wo, w1=W1, w2=W2, qs_r=qs_r, ks_r=ks_r,
            g_r=g_r, bln_r=bln_r, b2_r=b2_r, b1a_r=b1a_r, b1g_r=b1g_r,
            cs=cs, ss=ss, ci=ci, si=si, masks=mk,
            x_q=np.ascontiguousarray(xT[:, start:start + QT]),
            pmask=pm, ones2c_in=o2c, ones2r_in=o2r,
        ))
    return in_maps


_CACHE = {}


def kernel(**inputs):
    in_maps = _host_prepare(inputs)
    key = NL
    if key not in _CACHE:
        _CACHE[key] = build_program(NL)
    nc, _ = _CACHE[key]
    res = run_bass_kernel_spmd(nc, in_maps, core_ids=list(range(8)))
    out = np.empty((2, T, DM), np.float32)
    for r in range(8):
        b, q = r // 4, r % 4
        out[b, q * QT:(q + 1) * QT, :] = np.asarray(res.results[r]["o_out"]).T
    return out

